# version 20
# speedup vs baseline: 1.3764x; 1.0212x over previous
import numpy as np
import ml_dtypes
import concourse.bass as bass
import concourse.mybir as mybir
from concourse.bass_utils import run_bass_kernel_spmd

F32 = mybir.dt.float32
BF16 = mybir.dt.bfloat16
I32 = mybir.dt.int32
AF = mybir.ActivationFunctionType
ALU = mybir.AluOpType
AX = mybir.AxisListType

SEQ, D, H, DH = 4096, 1024, 16, 64
NT = SEQ // 128
NQ = 16
NLAT = 10
CTXT = 12
EPS = 1e-5
DBG = ()


def slot_kts(s):
    if s < NLAT:
        return list(range(12 + 2 * s + 2))
    return list(range(CTXT))


class P:
    def __init__(self, nc):
        self.nc = nc
        self.sems = {}
        self.cnt = {}
        self.lastw = {}

    def mk(self, es, name):
        self.sems[name] = es.enter_context(self.nc.semaphore(name))
        self.cnt[name] = 0

    def inc(self, ins, name, step=1):
        ins.then_inc(self.sems[name], step)
        self.cnt[name] += step
        return (name, self.cnt[name])

    def tok(self, name):
        return (name, self.cnt[name])

    def pre(self, name, step=1):
        self.cnt[name] += step
        return (name, self.cnt[name])

    def emit_pre(self, ins, name, step=1):
        ins.then_inc(self.sems[name], step)

    def wait(self, eng, ekey, tok):
        if tok is None:
            return
        name, val = tok
        if val <= 0:
            return
        k = (ekey, name)
        if self.lastw.get(k, -1) >= val:
            return
        eng.wait_ge(self.sems[name], val)
        self.lastw[k] = val


NONE_T = None


def build(dbg=DBG):
    nc = bass.Bass("TRN2", target_bir_lowering=False, debug=False)

    x_in = nc.dram_tensor("x_b", [SEQ, D], F32, kind="ExternalInput").ap()
    xq_in = nc.dram_tensor("x_q", [NQ * 128, D], F32, kind="ExternalInput").ap()
    xr_in = nc.dram_tensor("x_res", [NQ * 128, D], F32, kind="ExternalInput").ap()
    pos_in = nc.dram_tensor("pos_q", [1, NQ * 128], F32, kind="ExternalInput").ap()
    wq_in = nc.dram_tensor("Wq", [D, D], F32, kind="ExternalInput").ap()
    wkv_in = nc.dram_tensor("Wkv", [D, D], F32, kind="ExternalInput").ap()
    wo_in = nc.dram_tensor("Wo", [D, D], F32, kind="ExternalInput").ap()
    bo_in = nc.dram_tensor("bo", [1, D], F32, kind="ExternalInput").ap()
    lng_in = nc.dram_tensor("g_rep", [128, D], F32, kind="ExternalInput").ap()
    lnb_in = nc.dram_tensor("b_rep", [128, D], F32, kind="ExternalInput").ap()
    w1_in = nc.dram_tensor("w1", [4 * D, D], F32, kind="ExternalInput").ap()
    b1_in = nc.dram_tensor("b1c", [128, 32], F32, kind="ExternalInput").ap()
    w2_in = nc.dram_tensor("w2", [D, 4 * D], F32, kind="ExternalInput").ap()
    b2_in = nc.dram_tensor("b2", [1, D], F32, kind="ExternalInput").ap()
    id_in = nc.dram_tensor("ident", [128, 128], BF16, kind="ExternalInput").ap()
    ma_in = nc.dram_tensor("mask_a", [128, 128], BF16, kind="ExternalInput").ap()
    mb_in = nc.dram_tensor("mask_b", [128, 128], BF16, kind="ExternalInput").ap()
    iota_in = nc.dram_tensor("iota_r", [1, 128], F32, kind="ExternalInput").ap()
    onesf_in = nc.dram_tensor("ones_f", [1, 128], F32, kind="ExternalInput").ap()

    out_d = nc.dram_tensor("out", [NQ * 128, D], F32, kind="ExternalOutput").ap()
    y_d = nc.dram_tensor("y_sp", [NQ * 128, D], F32,
                         kind="ExternalOutput" if "y" in dbg else "Internal").ap()
    h_d = nc.dram_tensor("h_sp", [4 * D, NQ * 128], F32,
                         kind="ExternalOutput" if "h" in dbg else "Internal").ap()
    cc1i = nc.dram_tensor("cc1i", [1, 16], F32)
    cc1o = nc.dram_tensor("cc1o", [1, 16], F32)
    cc2i = nc.dram_tensor("cc2i", [1, 16], F32)
    cc2o = nc.dram_tensor("cc2o", [1, 16], F32)
    dbg_outs = {}
    for nm, shape, dt in [
        ("lkv", [128, NT * 1024], BF16), ("qfm", [128, NQ * 1024], BF16),
        ("avfm", [128, 16384], BF16), ("xqd", [128, 16384], BF16),
        ("scal", [1, 16], F32),
    ]:
        if nm in dbg:
            dbg_outs[nm] = nc.dram_tensor("dbg_" + nm, shape, dt, kind="ExternalOutput").ap()

    import contextlib
    es = contextlib.ExitStack()
    sb = lambda n, s, d: es.enter_context(nc.sbuf_tensor(n, s, d))

    AB = sb("AB", [128, 83456], BF16)
    R1, R2, R3, R4, QSTG = 0, 33280, 49664, 66048, 82432
    LKVS = 1040
    XROT, QXROT, PSB, LKVF, WOT = R4, R4 + 2048, R4 + 3072, R4 + 4096, R4 + 8192
    WQT, WKVT = R3, R3 + 8192
    HQ0, HQ1 = R1, R1 + 8192

    ctb = [sb("ctb0", [128, 32], F32)] * 2
    stb = [sb("stb0", [128, 32], F32)] * 2
    ntb = [sb("ntb0", [128, 32], F32)] * 2
    sc_a = sb("sc_a", [128, 1024], F32)
    sc_i = sb("sc_i", [128, 256], I32)
    sc_i2 = sb("sc_i2", [128, 256], I32)
    zb = sb("zb", [128, 1024], F32)
    yb = [sb(f"yb{i}", [128, 1024], F32) for i in range(2)]
    xob = [sb(f"xob{i}", [128, 1024], F32) for i in range(2)]
    wst = [sb(f"wst{i}", [128, 1024], F32) for i in range(2)]
    hb = [sb(f"hb{i}", [128, 256], F32) for i in range(2)]
    ident = sb("identb", [128, 128], BF16)
    identf = sb("identf", [128, 128], F32)
    mask_a = sb("mask_ab", [128, 128], BF16)
    mask_b = sb("mask_bb", [128, 128], BF16)
    iota_r = sb("iota_rb", [1, 128], F32)
    ones_f = sb("ones_fb", [1, 128], F32)
    ones_b = sb("ones_bb", [1, 128], BF16)
    onec_b = sb("onec_bb", [128, 1], BF16)
    b1c = sb("b1cb", [128, 32], F32)
    b1s = sb("b1sb", [128, 32], F32)
    gbb = sb("gbb", [128, D], BF16)
    bbb = sb("bbb", [128, D], BF16)
    eps_c = sb("eps_c", [128, 1], F32)
    invf = sb("invf", [1, 32], F32)
    c1 = sb("c1", [1, 1], F32)
    c2 = sb("c2", [1, 1], F32)
    twopi = sb("twopi", [1, 1], F32)
    r2pi = sb("r2pi", [1, 1], F32)

    r2pi_b = sb("r2pi_b", [128, 1], F32)
    twopi_b = sb("twopi_b", [128, 1], F32)
    mu8 = sb("mu8", [128, 8], F32)
    s28 = sb("s28", [128, 8], F32)
    var8 = sb("var8", [128, 8], F32)
    sd8 = sb("sd8", [128, 8], F32)
    rstd8 = sb("rstd8", [128, 8], F32)
    rec_q = sb("rec_q", [128, 1], F32)
    sum1 = sb("sum1", [128, 1], F32)
    s2c = sb("s2c", [128, 1], F32)
    mu1 = sb("mu1", [128, 1], F32)
    var1 = sb("var1", [128, 1], F32)
    sd1 = sb("sd1", [128, 1], F32)
    rstd1 = sb("rstd1", [128, 1], F32)
    tmp1 = sb("tmp1", [128, 1], F32)
    ymax = sb("ymax", [128, 1], F32)
    hmax = sb("hmax", [128, 1], F32)
    wmax1 = sb("wmax1", [128, 1], F32)
    wmax2 = sb("wmax2", [128, 1], F32)
    rmax = sb("rmax", [128, 1], F32)
    row16 = sb("row16", [1, 16], F32)
    gm1 = sb("gm1", [1, 1], F32)
    gm2 = sb("gm2", [1, 1], F32)
    wm1 = sb("wm1", [1, 1], F32)
    wm2 = sb("wm2", [1, 1], F32)
    sx_s = sb("sx_s", [1, 1], F32)
    sh_s = sb("sh_s", [1, 1], F32)
    s1_s = sb("s1_s", [1, 1], F32)
    s2_s = sb("s2_s", [1, 1], F32)
    rsx = sb("rsx", [1, 1], F32)
    rsh = sb("rsh", [1, 1], F32)
    rsw1 = sb("rsw1", [1, 1], F32)
    rsw2 = sb("rsw2", [1, 1], F32)
    rsx_b = sb("rsx_b", [128, 1], F32)
    rsh_b = sb("rsh_b", [128, 1], F32)
    rsw1_b = sb("rsw1_b", [128, 1], F32)
    rsw2_b = sb("rsw2_b", [128, 1], F32)
    s1_b = sb("s1_b", [128, 1], F32)
    s2_b = sb("s2_b", [128, 1], F32)

    pb = [es.enter_context(nc.psum_tensor(f"pb{i}", [128, 512], F32)) for i in range(6)]
    pbT = [es.enter_context(nc.psum_tensor(f"pbT{i}", [128, 1024], BF16)) for i in range(2)]

    p = P(nc)
    for s in ["dS", "dP", "dA", "v", "t", "s", "g", "cc",
              "wst0", "wst1", "xob0", "xob1", "yb0", "yb1", "hb0", "hb1",
              "wst0P", "wst1P", "xob0P", "xob1P", "hb2", "hb3"]:
        p.mk(es, s)

    with es, nc.Block() as block:
        @block.sync
        def _(sync):
            for dst, src in [(ident, id_in), (mask_a, ma_in), (mask_b, mb_in),
                             (iota_r, iota_in), (ones_f, onesf_in),
                             (b1c, b1_in)]:
                p.inc(sync.dma_start(out=dst[:], in_=src), "dS", 16)
        tok_ld0 = p.tok("dS")

        @block.vector
        def _(vector):
            p.wait(vector, "v", tok_ld0)

            p.inc(vector.memset(ones_b[:], 1.0), "v")
            p.inc(vector.memset(onec_b[:], 1.0), "v")
            p.inc(vector.memset(eps_c[:], EPS), "v")
            p.inc(vector.memset(c1[:], 10000.0), "v")
            p.inc(vector.memset(c2[:], 1.0), "v")
            p.inc(vector.tensor_copy(out=identf[:], in_=ident[:]), "v")
        tok_consts_v = p.tok("v")
        tok_idf = tok_consts_v

        @block.scalar
        def _(scalar):
            p.wait(scalar, "s", tok_consts_v)
            p.inc(scalar.activation(out=c1[:], in_=c1[:], func=AF.Ln), "s")
            p.inc(scalar.activation(out=c2[:], in_=c2[:], func=AF.Arctan), "s")
        tok_lnatan = p.tok("s")

        @block.vector
        def _(vector):
            p.wait(vector, "v", tok_lnatan)
            p.inc(vector.tensor_scalar(out=twopi[:], in0=c2[:], scalar1=8.0, scalar2=None,
                                       op0=ALU.mult), "v")
            p.wait(vector, "v", p.tok("v"))
            p.inc(vector.reciprocal(out=r2pi[:], in_=twopi[:]), "v")
            p.inc(vector.tensor_scalar(out=invf[:], in0=iota_r[0:1, 0:32],
                                       scalar1=c1[0:1, 0:1], scalar2=-1.0 / 32.0,
                                       op0=ALU.mult, op1=ALU.mult), "v")
        tok_invf_pre = p.tok("v")

        @block.scalar
        def _(scalar):
            p.wait(scalar, "s", tok_invf_pre)
            p.inc(scalar.activation(out=invf[:], in_=invf[:], func=AF.Exp), "s")
        tok_invf = p.tok("s")

        @block.sync
        def _(sync):
            p.wait(sync, "sy", tok_consts_v)
            p.inc(sync.dma_start(out=wst[0][:], in_=lng_in), "dS", 16)
            p.inc(sync.dma_start(out=wst[1][:], in_=lnb_in), "dS", 16)
        tok_gbld = p.tok("dS")

        @block.tensor
        def _(tensor):
            p.wait(tensor, "t", tok_invf)
            p.inc(tensor.matmul(pb[5][:, 0:1], ones_f[:], r2pi[:], start=True, stop=True), "t")
            p.inc(tensor.matmul(pb[5][:, 1:2], ones_f[:], twopi[:], start=True, stop=True), "t")
        tok_bc_mm = p.tok("t")

        @block.vector
        def _(vector):
            p.wait(vector, "v", tok_bc_mm)
            p.wait(vector, "v", tok_gbld)
            p.inc(vector.memset(
                AB[:, R1:R1 + NT * LKVS].rearrange("p (g d) -> p g d", d=65)[:, :, 64:65], 1.0), "v")
            p.inc(vector.tensor_copy(out=r2pi_b[:], in_=pb[5][:, 0:1]), "v")
            p.inc(vector.tensor_copy(out=twopi_b[:], in_=pb[5][:, 1:2]), "v")
            p.inc(vector.tensor_copy(out=gbb[:], in_=wst[0][:]), "v")
            p.inc(vector.tensor_copy(out=bbb[:], in_=wst[1][:]), "v")
        tok_bc = p.tok("v")

        tok_tbl_mm = {}
        tok_tbl_sin = {}

        def emit_table_tile(kind, tt, li):
            if kind == "kv":
                @block.vector
                def _(vector, tt=tt):
                    p.wait(vector, "v", tok_bc)
                    p.wait(vector, "v", p.tok("t"))
                    p.inc(vector.tensor_scalar(out=yb[0][0:1, 64:192], in0=iota_r[:],
                                               scalar1=float(128 * tt), scalar2=None,
                                               op0=ALU.add), "v")
                trow_tok = p.tok("v")
            else:
                @block.scalar
                def _(scalar, tt=tt, li=li):
                    p.wait(scalar, "s", tok_tbl_cons.get(li - 1))
                    p.inc(scalar.dma_start(out=yb[0][0:1, 64:192],
                                           in_=pos_in[0:1, tt * 128:(tt + 1) * 128]), "dA", 16)
                trow_tok = p.tok("dA")

            @block.tensor
            def _(tensor, trow_tok=trow_tok):
                p.wait(tensor, "tq", trow_tok)
                p.wait(tensor, "tq", tok_bc)
                p.wait(tensor, "tq", tok_tbl_cons.get(li - 1))
                p.inc(tensor.matmul(pb[5][:, 64:96], yb[0][0:1, 64:192], invf[:],
                                    start=True, stop=True), "t")
            mm_tok = p.tok("t")
            tok_tbl_mm[li] = mm_tok

            for off, dstb in ((0.25, ctb[li % 2]), (0.0, stb[li % 2])):
                prev_sin = p.tok("s")

                @block.vector
                def _(vector, off=off, mm_tok=mm_tok, li=li, prev_sin=prev_sin):
                    p.wait(vector, "v", mm_tok)
                    p.wait(vector, "v", prev_sin)
                    p.wait(vector, "v", tok_tbl_free.get(li - 1))
                    p.inc(vector.tensor_scalar(out=yb[0][:, 0:32], in0=pb[5][:, 64:96],
                                               scalar1=r2pi_b[:, 0:1], scalar2=off,
                                               op0=ALU.mult, op1=ALU.add), "v")
                    p.wait(vector, "v", p.tok("v"))
                    p.inc(vector.tensor_scalar(out=sc_i[:, 0:32], in0=yb[0][:, 0:32], scalar1=0.5,
                                               scalar2=None, op0=ALU.add), "v")
                    p.wait(vector, "v", p.tok("v"))
                    p.inc(vector.tensor_tensor(out=sc_i2[:, 0:32], in0=sc_i[:, 0:32],
                                               in1=yb[0][:, 0:32], op=ALU.subtract), "v")
                    p.wait(vector, "v", p.tok("v"))
                    p.inc(vector.scalar_tensor_tensor(out=yb[0][:, 32:64], in0=sc_i[:, 0:32],
                                                      scalar=-1.0, in1=yb[0][:, 0:32],
                                                      op0=ALU.mult, op1=ALU.add), "v")
                    p.wait(vector, "v", p.tok("v"))
                    p.inc(vector.tensor_tensor(out=yb[0][:, 32:64], in0=yb[0][:, 32:64], in1=sc_i2[:, 0:32],
                                               op=ALU.add), "v")
                    p.wait(vector, "v", p.tok("v"))
                    p.inc(vector.tensor_scalar(out=yb[0][:, 32:64], in0=yb[0][:, 32:64],
                                               scalar1=twopi_b[:, 0:1], scalar2=None,
                                               op0=ALU.mult), "v")
                arg_tok = p.tok("v")

                @block.scalar
                def _(scalar, dstb=dstb, arg_tok=arg_tok):
                    p.wait(scalar, "s", arg_tok)
                    p.inc(scalar.activation(out=dstb[:], in_=yb[0][:, 32:64], func=AF.Sin), "s")
            sin_done = p.tok("s")

            @block.vector
            def _(vector, li=li, sin_done=sin_done):
                p.wait(vector, "v", sin_done)
                p.inc(vector.tensor_scalar(out=ntb[li % 2][:], in0=stb[li % 2][:],
                                           scalar1=-1.0, scalar2=None, op0=ALU.mult), "v")
            tok_tbl_sin[li] = p.tok("v")
            tok_tbl_cons[li] = p.tok("v")

        tok_wcast, tok_wtr_t, tok_wevac = {}, {}, {}
        wlin = [0]
        for w_in, dstoff, scale in ((wq_in, WQT, 0.125), (wkv_in, WKVT, 1.0),
                                    (wo_in, WOT, 1.0)):
            for r in range(8):
                li = wlin[0]; wlin[0] += 1
                buf = wst[li % 2]

                @block.sync
                def _(sync, li=li, buf=buf, w_in=w_in, r=r):
                    p.wait(sync, "sy", tok_wcast.get(li - 1))
                    if li < 2:
                        p.wait(sync, "sy", tok_bc)
                    p.inc(sync.dma_start(out=buf[:], in_=w_in[r * 128:(r + 1) * 128, :]),
                          f"wst{li % 2}", 16)
                ltok = p.tok(f"wst{li % 2}")

                @block.vector
                def _(vector, li=li, buf=buf, ltok=ltok, scale=scale):
                    p.wait(vector, "v", ltok)
                    p.wait(vector, "v", tok_wtr_t.get(li - 1))
                    if scale != 1.0:
                        p.inc(vector.tensor_scalar(out=AB[:, QXROT:QXROT + 1024], in0=buf[:],
                                                   scalar1=scale, scalar2=None, op0=ALU.mult), "v")
                    else:
                        p.inc(vector.tensor_copy(out=AB[:, QXROT:QXROT + 1024], in_=buf[:]), "v")
                tok_wcast[li] = p.tok("v")
                ctok = p.tok("v")

                for c in range(8):
                    @block.tensor
                    def _(tensor, li=li, c=c, ctok=ctok):
                        p.wait(tensor, "t", ctok)
                        p.wait(tensor, "t", tok_wevac.get(li * 8 + c - 2))
                        p.inc(tensor.transpose(pbT[c % 2][:, 0:128],
                                               AB[:, QXROT + c * 128:QXROT + (c + 1) * 128],
                                               ident[:]), "t")
                    tok_wtr_t[(li, c)] = p.tok("t")

                    @block.vector
                    def _(vector, li=li, c=c, r=r, dstoff=dstoff):
                        p.wait(vector, "v", tok_wtr_t[(li, c)])
                        p.inc(vector.tensor_copy(
                            out=AB[:, dstoff + c * 1024 + r * 128: dstoff + c * 1024 + (r + 1) * 128],
                            in_=pbT[c % 2][:, 0:128]), "v")
                    tok_wevac[li * 8 + c] = p.tok("v")
                tok_wtr_t[li] = p.tok("t")
        tok_weights = p.tok("v")

        tok_xcast, tok_xtr_t, tok_xfm_ev, tok_projmm = {}, {}, {}, {}
        tok_rope_psum_free, tok_lkv, tok_qtr_t, tok_qfm = {}, {}, {}, {}
        tok_rot_cons = {}
        tok_qfm_lin = {}
        tok_tbl_cons = {}
        tok_tbl_free = {}
        plin = [0]

        def emit_proj_tile(kind, tt):
            li = plin[0]; plin[0] += 1
            key = (kind, tt)
            emit_table_tile(kind, tt, li)
            buf = wst[li % 2]
            src = x_in[tt * 128:(tt + 1) * 128, :] if kind == "kv" else \
                xq_in[tt * 128:(tt + 1) * 128, :]
            xroto = QSTG

            @block.sync
            def _(sync, li=li, buf=buf, src=src):
                p.wait(sync, "sy", tok_xcast.get(li - 1))
                if li < 2:
                    p.wait(sync, "sy", tok_weights)
                p.inc(sync.dma_start(out=buf[:], in_=src), f"wst{li % 2}", 16)
            ltok = p.tok(f"wst{li % 2}")

            @block.vector
            def _(vector, li=li, buf=buf, ltok=ltok, xroto=xroto):
                p.wait(vector, "v", ltok)
                p.wait(vector, "v", tok_xtr_t.get(li - 1))
                p.inc(vector.tensor_copy(out=AB[:, xroto:xroto + 1024], in_=buf[:]), "v")
            tok_xcast[li] = p.tok("v")

            for c in range(8):
                @block.tensor
                def _(tensor, li=li, c=c, xroto=xroto):
                    p.wait(tensor, "t", tok_xcast[li])
                    p.wait(tensor, "t", tok_xfm_ev.get(li * 8 + c - 2))
                    p.inc(tensor.transpose(pbT[c % 2][:, 128:256],
                                           AB[:, xroto + c * 128:xroto + (c + 1) * 128],
                                           ident[:]), "t")
                tok_xtr_t[(li, c)] = p.tok("t")

                @block.vector
                def _(vector, li=li, c=c):
                    p.wait(vector, "v", tok_xtr_t[(li, c)])
                    if c == 0:
                        p.wait(vector, "v", tok_projmm.get(li - 1))
                    p.inc(vector.tensor_copy(out=AB[:, QXROT + c * 128:QXROT + (c + 1) * 128],
                                             in_=pbT[c % 2][:, 128:256]), "v")
                tok_xfm_ev[li * 8 + c] = p.tok("v")
            tok_xtr_t[li] = p.tok("t")
            fm_tok = p.tok("v")

            wt_off = WKVT if kind == "kv" else WQT

            @block.tensor
            def _(tensor, li=li, fm_tok=fm_tok, wt_off=wt_off):
                p.wait(tensor, "t", fm_tok)
                p.wait(tensor, "t", tok_weights)
                p.wait(tensor, "t", tok_rope_psum_free.get(li - 1))
                for c in range(8):
                    lhsT = AB[:, QXROT + c * 128:QXROT + (c + 1) * 128]
                    p.inc(tensor.matmul(pb[0][:, 0:512], lhsT,
                                        AB[:, wt_off + c * 1024:wt_off + c * 1024 + 512],
                                        start=(c == 0), stop=(c == 7)), "t")
                    p.inc(tensor.matmul(pb[1][:, 0:512], lhsT,
                                        AB[:, wt_off + c * 1024 + 512:wt_off + (c + 1) * 1024],
                                        start=(c == 0), stop=(c == 7)), "t")
            tok_projmm[li] = p.tok("t")

            for half in range(2):
                bank = pb[half]
                ps4 = bank[:].rearrange("p (h s d) -> p h s d", s=2, d=32)
                ps3 = bank[:].rearrange("p (hs d) -> p hs d", d=32)
                cslice = ctb[li % 2][:]
                sslice = stb[li % 2][:]

                @block.vector
                def _(vector, key=key, half=half, li=li, ps3=ps3, ps4=ps4,
                      cslice=cslice, sslice=sslice, kind=kind, tt=tt):
                    p.wait(vector, "v", tok_projmm[li])
                    p.wait(vector, "v", tok_rot_cons.get(li * 2 + half))
                    p.wait(vector, "v", tok_tbl_sin[li])
                    if kind == "q":
                        p.wait(vector, "v", tok_qtr_t.get(("q", tt, half - 1)))
                        if half == 0:
                            p.wait(vector, "v", tok_qtr_t.get(("q", tt - 1, 1)))
                    rot4 = sc_a[:, 0:512].rearrange("p (h s d) -> p h s d", s=2, d=32)
                    rot3 = sc_a[:, 0:512].rearrange("p (hs d) -> p hs d", d=32)
                    t13 = sc_a[:, 512:1024].rearrange("p (hs d) -> p hs d", d=32)
                    nslice = ntb[li % 2][:]
                    p.inc(vector.tensor_tensor(out=rot4[:, :, 0, :], in0=ps4[:, :, 1, :],
                                               in1=nslice[:, None, :].broadcast_to((128, 8, 32)),
                                               op=ALU.mult), "v")
                    p.inc(vector.tensor_tensor(out=rot4[:, :, 1, :], in0=ps4[:, :, 0, :],
                                               in1=sslice[:, None, :].broadcast_to((128, 8, 32)),
                                               op=ALU.mult), "v")
                    p.inc(vector.tensor_tensor(out=t13, in0=ps3,
                                               in1=cslice[:, None, :].broadcast_to((128, 16, 32)),
                                               op=ALU.mult), "v")
                    p.wait(vector, "v", p.tok("v"))
                    if kind == "q":
                        p.inc(vector.tensor_tensor(
                            out=AB[:, PSB:PSB + 512].rearrange("p (hs d) -> p hs d", d=32),
                            in0=t13, in1=rot3, op=ALU.add), "v")
                    else:
                        p.inc(vector.tensor_tensor(
                            out=zb[:, half * 512:half * 512 + 512].rearrange("p (hs d) -> p hs d", d=32),
                            in0=t13, in1=rot3, op=ALU.add), "v")
                tok_roped = p.tok("v")
                tok_rot_cons[li * 2 + half + 1] = tok_roped
                if half == 1:
                    tok_tbl_free[li] = tok_roped

                if kind == "kv":
                    zslice = zb[:, half * 512:half * 512 + 512]
                    z3 = zslice.rearrange("p (h d) -> p h d", d=64)

                    @block.vector
                    def _(vector, z3=z3, zslice=zslice, tok_roped=tok_roped):
                        p.wait(vector, "v", tok_roped)
                        p.inc(vector.tensor_tensor(out=sc_a[:, 0:512], in0=zslice,
                                                   in1=zslice, op=ALU.mult), "v")
                        p.wait(vector, "v", p.tok("v"))
                        p.inc(vector.reduce_sum(out=mu8[:, 0:8], in_=z3, axis=AX.X), "v")
                        p.inc(vector.reduce_sum(out=s28[:, 0:8],
                                                in_=sc_a[:, 0:512].rearrange("p (h d) -> p h d", d=64),
                                                axis=AX.X), "v")
                        p.wait(vector, "v", p.tok("v"))
                        p.inc(vector.tensor_scalar(out=mu8[:, 0:8], in0=mu8[:, 0:8],
                                                   scalar1=1.0 / 64, scalar2=None, op0=ALU.mult), "v")
                        p.wait(vector, "v", p.tok("v"))
                        p.inc(vector.scalar_tensor_tensor(out=var8[:, 0:8], in0=mu8[:, 0:8],
                                                          scalar=1.0, in1=mu8[:, 0:8],
                                                          op0=ALU.mult, op1=ALU.mult), "v")
                        p.inc(vector.tensor_scalar(out=s28[:, 0:8], in0=s28[:, 0:8],
                                                   scalar1=1.0 / 64, scalar2=None, op0=ALU.mult), "v")
                        p.wait(vector, "v", p.tok("v"))
                        p.inc(vector.tensor_tensor(out=var8[:, 0:8], in0=s28[:, 0:8],
                                                   in1=var8[:, 0:8], op=ALU.subtract), "v")
                    tok_var = p.tok("v")

                    @block.scalar
                    def _(scalar, tok_var=tok_var):
                        p.wait(scalar, "s", tok_var)
                        p.inc(scalar.activation(out=sd8[:, 0:8], in_=var8[:, 0:8],
                                                func=AF.Sqrt, bias=eps_c[:, 0:1]), "s")
                    tok_sd = p.tok("s")

                    @block.vector
                    def _(vector, tt=tt, half=half, z3=z3, zslice=zslice, tok_sd=tok_sd):
                        p.wait(vector, "v", tok_sd)
                        p.inc(vector.reciprocal(out=rstd8[:, 0:8], in_=sd8[:, 0:8]), "v")
                        p.wait(vector, "v", p.tok("v"))
                        p.inc(vector.tensor_tensor(out=z3, in0=z3,
                                                   in1=mu8[:, 0:8, None].broadcast_to((128, 8, 64)),
                                                   op=ALU.subtract), "v")
                        p.wait(vector, "v", p.tok("v"))
                        p.inc(vector.tensor_tensor(out=z3, in0=z3,
                                                   in1=rstd8[:, 0:8, None].broadcast_to((128, 8, 64)),
                                                   op=ALU.mult), "v")
                        p.wait(vector, "v", p.tok("v"))
                        p.inc(vector.tensor_tensor(out=z3, in0=z3,
                                                   in1=gbb[:, half * 512:half * 512 + 512]
                                                   .rearrange("p (h d) -> p h d", d=64),
                                                   op=ALU.mult), "v")
                        p.wait(vector, "v", p.tok("v"))
                        p.inc(vector.tensor_tensor(
                            out=AB[:, R1 + tt * LKVS + half * 520: R1 + tt * LKVS + half * 520 + 520]
                            .rearrange("p (h d) -> p h d", d=65)[:, :, 0:64],
                            in0=zslice.rearrange("p (h d) -> p h d", d=64),
                            in1=bbb[:, half * 512:half * 512 + 512].rearrange("p (h d) -> p h d", d=64),
                            op=ALU.add), "v")
                    tok_lkv[(tt, half)] = p.tok("v")
                    if half == 1:
                        tok_rope_psum_free[li] = p.tok("v")
                else:
                    if half == 1:
                        tok_rope_psum_free[li] = tok_roped

                    for c in range(4):
                        cc = half * 4 + c

                        @block.tensor
                        def _(tensor, tt=tt, half=half, c=c, cc=cc, tok_roped=tok_roped):
                            p.wait(tensor, "t", tok_roped)
                            p.wait(tensor, "t", tok_qfm_lin.get(tt * 8 + cc - 2))
                            p.inc(tensor.transpose(pbT[cc % 2][:, 256:384],
                                                   AB[:, PSB + c * 128:PSB + (c + 1) * 128],
                                                   ident[:]), "t")
                        tok_qtr_t[("q", tt, half, c)] = p.tok("t")

                        @block.vector
                        def _(vector, tt=tt, cc=cc, half=half, c=c):
                            p.wait(vector, "v", tok_qtr_t[("q", tt, half, c)])
                            p.inc(vector.tensor_copy(
                                out=AB[:, R2 + cc * 2048 + tt * 128: R2 + cc * 2048 + (tt + 1) * 128],
                                in_=pbT[cc % 2][:, 256:384]), "v")
                        tok_qfm[("q", tt, cc)] = p.tok("v")
                        tok_qfm_lin[tt * 8 + cc] = p.tok("v")
                    tok_qtr_t[("q", tt, half)] = p.tok("t")
        tok_p1_v = None

        if "lkv" in dbg:
            @block.sync
            def _(sync):
                p.wait(sync, "sy", tok_p1_v)
                p.inc(sync.dma_start(out=dbg_outs["lkv"], in_=AB[:, R1:R1 + NT * 1024]), "dS", 16)
        if "qfm" in dbg:
            @block.sync
            def _(sync):
                p.wait(sync, "sy", tok_p1_v)
                p.inc(sync.dma_start(out=dbg_outs["qfm"], in_=AB[:, R2:R2 + NQ * 1024]), "dS", 16)

        tok_lkvf_t, tok_lkvf_ev = {}, {}
        tok_scmm, tok_exp, tok_avmm_g, tok_avn = {}, {}, {}, {}
        tok_rcb = {}
        g_all = [0]
        av_lin = [0]
        tok_pair_scores_t = {}
        SLOTP = [XROT, XROT + 512, XROT + 1024]

        def pair_kts(j):
            return 16 + 4 * j if j < 5 else CTXT


        def emit_pair_tr(pr, kt_lo, kt_hi):
            for kt in range(kt_lo, kt_hi):
                @block.tensor
                def _(tensor, pr=pr, kt=kt):
                    p.wait(tensor, "t", tok_lkv.get((kt, pr // 4)))
                    p.wait(tensor, "t", tok_lkvf_ev.get((pr, kt - 2)))
                    for hh in range(2):
                        co = R1 + kt * LKVS + (2 * pr + hh) * 65
                        p.inc(tensor.transpose(
                            pbT[kt % 2][64 * hh:64 * hh + 64, 384:512],
                            AB[:, co:co + 64], ident[:]), "t")
                tok_lkvf_t[(pr, kt)] = p.tok("t")

                @block.vector
                def _(vector, pr=pr, kt=kt):
                    p.wait(vector, "v", tok_lkvf_t[(pr, kt)])
                    if kt == 0 and pr > 0:
                        p.wait(vector, "v", tok_pair_scores_t[pr - 1])
                    p.inc(vector.tensor_copy(out=AB[:, LKVF + kt * 128:LKVF + (kt + 1) * 128],
                                             in_=pbT[kt % 2][:, 384:512]), "v")
                tok_lkvf_ev[(pr, kt)] = p.tok("v")

        def emit_pair_strands(pr, j_list, tr_after=None):
            exp_defer = []
            norm_defer = [None]
            for j in j_list:
                if tr_after and j in tr_after:
                    emit_pair_tr(pr, *tr_after[j])
                nkt = pair_kts(j)
                qcol = R2 + pr * 2048 + j * 256
                for hh in range(2):
                    hd = 2 * pr + hh
                    base = 64 * hh
                    al = av_lin[0]; av_lin[0] += 1
                    av_bank = pb[2 + (al % 2)]
                    groups = []
                    for kt0 in range(0, nkt, 2):
                        kts = [kt0] + ([kt0 + 1] if kt0 + 1 < nkt else [])
                        g = g_all[0]; g_all[0] += 1
                        sc_bank = [pb[0], pb[1], pb[4]][g % 3]
                        poff = SLOTP[g % 3]
                        groups.append((g, kts, sc_bank, poff))

                    def emit_score(gi):
                        g, kts, sc_bank, poff = groups[gi]

                        @block.tensor
                        def _(tensor, pr=pr, base=base, kts=kts, g=g,
                              sc_bank=sc_bank, qcol=qcol):
                            p.wait(tensor, "t", tok_lkvf_ev[(pr, kts[-1])])
                            p.wait(tensor, "t", tok_qfm[("q", 2 * j + 1, pr)])
                            p.wait(tensor, "t", tok_exp.get(g - 3))
                            for i, kt in enumerate(kts):
                                p.inc(tensor.matmul(
                                    sc_bank[:, i * 256:(i + 1) * 256],
                                    AB[base:base + 64, LKVF + kt * 128:LKVF + (kt + 1) * 128],
                                    AB[base:base + 64, qcol:qcol + 256],
                                    start=True, stop=True), "t")
                        tok_scmm[g] = p.tok("t")
                        tok_exp[g] = p.pre("s")
                        exp_defer.append((g, len(kts), sc_bank, poff))

                    def emit_av(gi):
                        g, kts, sc_bank, poff = groups[gi]
                        mtok = tok_exp[g]
                        if j < 5:
                            orig0 = 12 + 4 * j
                            for i, kt in enumerate(kts):
                                for sl in range(2):
                                    mm = None
                                    if sl == 0 and kt == orig0:
                                        mm = mask_a
                                    elif sl == 0 and kt == orig0 + 1:
                                        mm = mask_b
                                    elif sl == 0 and kt >= orig0 + 2:
                                        mm = "zero"
                                    elif sl == 1 and kt == orig0 + 2:
                                        mm = mask_a
                                    elif sl == 1 and kt == orig0 + 3:
                                        mm = mask_b
                                    if mm is None:
                                        continue

                                    @block.vector
                                    def _(vector, i=i, sl=sl, poff=poff, mm=mm, g=g):
                                        p.wait(vector, "v", tok_exp[g])
                                        off = poff + i * 256 + sl * 128
                                        if mm == "zero":
                                            p.inc(vector.memset(AB[:, off:off + 128], 0.0), "v")
                                        else:
                                            p.inc(vector.tensor_tensor(
                                                out=AB[:, off:off + 128],
                                                in0=AB[:, off:off + 128],
                                                in1=mm[:], op=ALU.mult), "v")
                                    mtok = p.tok("v")

                        @block.tensor
                        def _(tensor, hd=hd, kts=kts, g=g, al=al, nkt=nkt,
                              av_bank=av_bank, mtok=mtok, poff=poff):
                            p.wait(tensor, "t", mtok)
                            p.wait(tensor, "t", tok_avn.get(al - 2))
                            for i, kt in enumerate(kts):
                                p.inc(tensor.matmul(
                                    av_bank[0:65, 0:256],
                                    AB[:, R1 + kt * LKVS + hd * 65: R1 + kt * LKVS + (hd + 1) * 65],
                                    AB[:, poff + i * 256:poff + (i + 1) * 256],
                                    start=(kt == 0), stop=(kt == nkt - 1)), "t")
                        tok_avmm_g[g] = p.tok("t")

                    ng = len(groups)
                    for gi in range(ng + 2):
                        if gi < ng:
                            emit_score(gi)
                        if gi == 2 and norm_defer[0] is not None:
                            norm_defer[0]()
                            norm_defer[0] = None
                        if gi >= 2:
                            emit_av(gi - 2)
                    tok_av_done = p.tok("t")

                    def emit_norm(hd=hd, hh=hh, pr=pr, j=j, al=al, av_bank=av_bank,
                                  tok_av_done=tok_av_done):
                        @block.vector
                        def _(vector, hd=hd, hh=hh, pr=pr, j=j, al=al, av_bank=av_bank,
                              tok_av_done=tok_av_done):
                            p.wait(vector, "v", tok_av_done)
                            p.wait(vector, "v", p.tok("v"))
                            p.inc(vector.reciprocal(out=yb[0][0:1, 192:448], in_=av_bank[64:65, 0:256]), "v")
                        tok_recp = p.tok("v")

                        @block.tensor
                        def _(tensor, al=al, tok_recp=tok_recp):
                            p.wait(tensor, "t", tok_recp)
                            p.wait(tensor, "t", tok_rcb.get(al - 1))
                            p.inc(tensor.matmul(pb[5][0:64, 256:512], ones_f[0:1, 0:64], yb[0][0:1, 192:448],
                                                start=True, stop=True), "t")
                        tok_rcmm = p.tok("t")

                        @block.vector
                        def _(vector, hd=hd, hh=hh, pr=pr, j=j, al=al, av_bank=av_bank,
                              tok_rcmm=tok_rcmm):
                            p.wait(vector, "v", tok_rcmm)
                            p.wait(vector, "v", p.tok("v"))
                            p.inc(vector.tensor_copy(out=sc_a[0:64, 0:256], in_=pb[5][0:64, 256:512]), "v")
                            p.wait(vector, "v", p.tok("v"))
                            dst = R3 + (pr) * 2048 + j * 256
                            p.inc(vector.tensor_tensor(
                                out=AB[64 * hh:64 * hh + 64, dst:dst + 256],
                                in0=av_bank[0:64, 0:256], in1=sc_a[0:64, 0:256],
                                op=ALU.mult), "v")
                        tok_avn[al] = p.tok("v")
                        tok_rcb[al] = p.tok("v")
                    norm_defer[0] = emit_norm
            if norm_defer[0] is not None:
                norm_defer[0]()
                norm_defer[0] = None

            @block.scalar
            def _(scalar, exp_defer=list(exp_defer)):
                for g, nk, bank, poff in exp_defer:
                    p.wait(scalar, "s", tok_scmm[g])
                    p.wait(scalar, "s", tok_avmm_g.get(g - 3))
                    p.emit_pre(scalar.activation(out=AB[:, poff:poff + nk * 256],
                                                 in_=bank[:, 0:nk * 256],
                                                 func=AF.Exp), "s")

        def emit_pair_finish(pr):
            tok_pair_scores_t[pr] = p.tok("t")

        for s in range(NQ):
            emit_proj_tile("q", s)
        for t in range(12):
            emit_proj_tile("kv", t)
        emit_pair_tr(0, 0, 12)
        emit_pair_strands(0, [5, 6, 7])
        for t0, jj in ((12, 0), (16, 1), (20, 2), (24, 3), (28, 4)):
            for t in range(t0, t0 + 4):
                emit_proj_tile("kv", t)
            emit_pair_tr(0, t0, t0 + 4)
            emit_pair_strands(0, [jj])
        emit_pair_finish(0)
        tok_p1_v = p.tok("v")
        for pr in range(1, 8):
            emit_pair_tr(pr, 0, 12)
            emit_pair_strands(pr, [5, 6, 7, 0, 1, 2, 3, 4],
                              tr_after={7: (12, 16), 0: (16, 22), 1: (22, 26),
                                        2: (26, 30), 3: (30, 32)})
            emit_pair_finish(pr)
        tok_p2_v = p.tok("v")

        @block.sync
        def _(sync):
            p.wait(sync, "sy", tok_p2_v)
            p.inc(sync.dma_start(out=yb[0][0:1, :], in_=bo_in), "yb0", 16)
        boldk = p.tok("yb0")

        @block.vector
        def _(vector):
            p.wait(vector, "v", boldk)
            p.inc(vector.tensor_copy(out=AB[0:1, LKVF:LKVF + 1024], in_=yb[0][0:1, :]), "v")
        tok_bo = p.tok("v")

        if "avfm" in dbg:
            @block.sync
            def _(sync):
                p.wait(sync, "sy", tok_p2_v)
                p.inc(sync.dma_start(out=dbg_outs["avfm"], in_=AB[:, R3:R3 + 16384]), "dS", 16)

        tok_womm, tok_z3, tok_sq3, tok_yv, tok_ydma = {}, {}, {}, {}, {}
        for s_ in range(NQ):
            ba, bb_ = (pb[4], pb[5]) if s_ % 2 == 0 else (pb[2], pb[3])

            @block.sync
            def _(sync, s_=s_):
                p.wait(sync, "sy", tok_z3.get(s_ - 1))
                if s_ == 0:
                    p.wait(sync, "sy", tok_p1_v)
                p.inc(sync.dma_start(out=xob[s_ % 2][:], in_=xr_in[s_ * 128:(s_ + 1) * 128, :]),
                      f"xob{s_ % 2}", 16)
            xtok = p.tok(f"xob{s_ % 2}")

            @block.tensor
            def _(tensor, s_=s_, ba=ba, bb_=bb_):
                p.wait(tensor, "t", tok_consts_v)
                p.wait(tensor, "t", tok_p2_v)
                p.wait(tensor, "t", tok_bo)
                for c in range(8):
                    p.wait(tensor, "t", tok_avn[c * 16 + (s_ // 2) * 2 + 0])
                    p.wait(tensor, "t", tok_avn[c * 16 + (s_ // 2) * 2 + 1])
                p.wait(tensor, "t", tok_z3.get(s_ - 2))
                for c in range(8):
                    lhsT = AB[:, R3 + c * 2048 + s_ * 128: R3 + c * 2048 + (s_ + 1) * 128]
                    p.inc(tensor.matmul(ba[:, 0:512], lhsT,
                                        AB[:, WOT + c * 1024:WOT + c * 1024 + 512],
                                        start=(c == 0), stop=False), "t")
                    p.inc(tensor.matmul(bb_[:, 0:512], lhsT,
                                        AB[:, WOT + c * 1024 + 512:WOT + (c + 1) * 1024],
                                        start=(c == 0), stop=False), "t")
                p.inc(tensor.matmul(ba[:, 0:512], ones_b[:], AB[0:1, LKVF:LKVF + 512],
                                    start=False, stop=True), "t")
                p.inc(tensor.matmul(bb_[:, 0:512], ones_b[:], AB[0:1, LKVF + 512:LKVF + 1024],
                                    start=False, stop=True), "t")
            tok_womm[s_] = p.tok("t")

            @block.vector
            def _(vector, s_=s_, xtok=xtok, ba=ba, bb_=bb_):
                p.wait(vector, "v", tok_womm[s_])
                p.wait(vector, "v", xtok)
                p.wait(vector, "v", tok_sq3.get(s_ - 1))
                xo = xob[s_ % 2]
                p.inc(vector.tensor_tensor(out=zb[:, 0:512], in0=ba[:, 0:512],
                                           in1=xo[:, 0:512], op=ALU.add), "v")
                p.inc(vector.tensor_tensor(out=zb[:, 512:1024], in0=bb_[:, 0:512],
                                           in1=xo[:, 512:1024], op=ALU.add), "v")
            tok_z3[s_] = p.tok("v")

            @block.vector
            def _(vector, s_=s_):
                p.wait(vector, "v", tok_z3[s_])
                p.inc(vector.scalar_tensor_tensor(out=sc_a[:], in0=zb[:], scalar=1.0,
                                                  in1=zb[:], op0=ALU.mult, op1=ALU.mult,
                                                  accum_out=s2c[:]), "v")
            tok_sq3[s_] = p.tok("v")

            @block.vector
            def _(vector, s_=s_):
                p.wait(vector, "v", tok_z3[s_])
                p.inc(vector.reduce_sum(out=sum1[:], in_=zb[:], axis=AX.X), "v")
                p.wait(vector, "v", tok_sq3[s_])
                p.wait(vector, "v", p.tok("v"))
                p.inc(vector.tensor_scalar(out=mu1[:], in0=sum1[:], scalar1=1.0 / D,
                                           scalar2=None, op0=ALU.mult), "v")
                p.wait(vector, "v", p.tok("v"))
                p.inc(vector.scalar_tensor_tensor(out=tmp1[:], in0=mu1[:], scalar=1.0,
                                                  in1=mu1[:], op0=ALU.mult, op1=ALU.mult), "v")
                p.inc(vector.tensor_scalar(out=var1[:], in0=s2c[:], scalar1=1.0 / D,
                                           scalar2=None, op0=ALU.mult), "v")
                p.wait(vector, "v", p.tok("v"))
                p.inc(vector.tensor_tensor(out=var1[:], in0=var1[:], in1=tmp1[:],
                                           op=ALU.subtract), "v")
            tok_var3 = p.tok("v")

            @block.scalar
            def _(scalar, tok_var3=tok_var3):
                p.wait(scalar, "s", tok_var3)
                p.inc(scalar.activation(out=sd1[:], in_=var1[:], func=AF.Sqrt,
                                        bias=eps_c[:, 0:1]), "s")
            tok_sd3 = p.tok("s")

            @block.vector
            def _(vector, s_=s_, tok_sd3=tok_sd3):
                p.wait(vector, "v", tok_sd3)
                p.wait(vector, "v", tok_ydma.get(s_ - 2))
                p.inc(vector.reciprocal(out=rstd1[:], in_=sd1[:]), "v")
                p.wait(vector, "v", p.tok("v"))
                yv = yb[s_ % 2]
                p.inc(vector.tensor_scalar(out=yv[:], in0=zb[:], scalar1=mu1[:, 0:1],
                                           scalar2=rstd1[:, 0:1], op0=ALU.subtract,
                                           op1=ALU.mult), "v")
                p.wait(vector, "v", p.tok("v"))
                p.wait(vector, "v", p.tok("v"))
                p.inc(vector.reduce_max(out=rmax[:], in_=yv[:], axis=AX.X,
                                        apply_absolute_value=True), "v")
                p.wait(vector, "v", p.tok("v"))
                if s_ == 0:
                    p.inc(vector.tensor_copy(out=ymax[:], in_=rmax[:]), "v")
                else:
                    p.inc(vector.tensor_tensor(out=ymax[:], in0=ymax[:], in1=rmax[:],
                                               op=ALU.max), "v")
            tok_yv[s_] = p.tok("v")

            @block.sync
            def _(sync, s_=s_):
                p.wait(sync, "sy", tok_yv[s_])
                p.inc(sync.dma_start(out=y_d[s_ * 128:(s_ + 1) * 128, :], in_=yb[s_ % 2][:]),
                      f"yb{s_ % 2}", 16)
            tok_ydma[s_] = p.tok(f"yb{s_ % 2}")
        tok_p3_v = p.tok("v")

        tok_wld, tok_wmaxv = {}, {}
        for i in range(64):
            @block.gpsimd
            def _(gpsimd, i=i):
                p.wait(gpsimd, "g", tok_wmaxv.get(i - 1))
                if i < 2:
                    p.wait(gpsimd, "g", tok_p1_v)
                if i < 32:
                    wsrc = w1_in[i * 128:(i + 1) * 128, :]
                else:
                    r, hc = divmod(i - 32, 4)
                    wsrc = w2_in[r * 128:(r + 1) * 128, hc * 1024:(hc + 1) * 1024]
                p.inc(gpsimd.dma_start(out=wst[i % 2][:], in_=wsrc), f"wst{i % 2}P", 16)
                tok_wld[i] = p.tok(f"wst{i % 2}P")

            @block.vector
            def _(vector, i=i):
                if i == 0:
                    p.wait(vector, "v", tok_p1_v)
                p.wait(vector, "v", tok_wld[i])
                p.wait(vector, "v", p.tok("v"))
                p.inc(vector.reduce_max(out=rmax[:], in_=wst[i % 2][:], axis=AX.X,
                                        apply_absolute_value=True), "v")
                p.wait(vector, "v", p.tok("v"))
                dst = wmax1 if i < 32 else wmax2
                if i in (0, 32):
                    p.inc(vector.tensor_copy(out=dst[:], in_=rmax[:]), "v")
                else:
                    p.inc(vector.tensor_tensor(out=dst[:], in0=dst[:], in1=rmax[:],
                                               op=ALU.max), "v")
                tok_wmaxv[i] = p.tok("v")
        tok_wmax = p.tok("v")



        @block.tensor
        def _(tensor):
            p.wait(tensor, "t", tok_p3_v)
            p.wait(tensor, "t", tok_wmax)
            p.wait(tensor, "t", tok_idf)
            p.inc(tensor.matmul(pb[0][0:1, 0:128], ymax[:], identf[:],
                                start=True, stop=True), "t")
            p.inc(tensor.matmul(pb[1][0:1, 0:128], wmax1[:], identf[:],
                                start=True, stop=True), "t")
            p.inc(tensor.matmul(pb[0][0:1, 128:256], wmax2[:], identf[:],
                                start=True, stop=True), "t")
        tok_pr_mm = p.tok("t")

        @block.vector
        def _(vector):
            p.wait(vector, "v", tok_pr_mm)
            p.inc(vector.reduce_max(out=gm1[:], in_=pb[0][0:1, 0:128], axis=AX.X), "v")
            p.inc(vector.reduce_max(out=wm1[:], in_=pb[1][0:1, 0:128], axis=AX.X), "v")
            p.inc(vector.reduce_max(out=wm2[:], in_=pb[0][0:1, 128:256], axis=AX.X), "v")
            p.wait(vector, "v", p.tok("v"))
            p.inc(vector.tensor_scalar(out=row16[:], in0=ones_f[0:1, 0:16],
                                       scalar1=gm1[0:1, 0:1], scalar2=None, op0=ALU.mult), "v")
        tok_gm1row = p.tok("v")

        @block.vector
        def _(vector):
            p.wait(vector, "v", tok_gm1row)
            p.inc(vector.tensor_scalar(out=wm1[:], in0=wm1[:], scalar1=1e-6, scalar2=None,
                                       op0=ALU.max), "v")
            p.inc(vector.tensor_scalar(out=wm2[:], in0=wm2[:], scalar1=1e-6, scalar2=None,
                                       op0=ALU.max), "v")
            p.wait(vector, "v", p.tok("v"))
            p.inc(vector.reciprocal(out=rsw1[:], in_=wm1[:]), "v")
            p.inc(vector.reciprocal(out=rsw2[:], in_=wm2[:]), "v")
            p.wait(vector, "v", p.tok("v"))
            p.inc(vector.tensor_scalar(out=rsw1[:], in0=rsw1[:], scalar1=127.0, scalar2=None,
                                       op0=ALU.mult), "v")
            p.inc(vector.tensor_scalar(out=rsw2[:], in0=rsw2[:], scalar1=127.0, scalar2=None,
                                       op0=ALU.mult), "v")
        tok_rsw = p.tok("v")

        @block.tensor
        def _(tensor):
            p.wait(tensor, "t", tok_rsw)
            p.wait(tensor, "t", tok_avn[127])
            p.inc(tensor.matmul(pb[2][:, 1:2], ones_f[:], rsw1[:], start=True, stop=True), "t")
            p.inc(tensor.matmul(pb[2][:, 2:3], ones_f[:], rsw2[:], start=True, stop=True), "t")
        tok_bcw_mm = p.tok("t")

        @block.vector
        def _(vector):
            p.wait(vector, "v", tok_bcw_mm)
            p.inc(vector.tensor_copy(out=rsw1_b[:], in_=pb[2][:, 1:2]), "v")
            p.inc(vector.tensor_copy(out=rsw2_b[:], in_=pb[2][:, 2:3]), "v")
        tok_bcw = p.tok("v")

        @block.gpsimd
        def _(gpsimd):
            p.wait(gpsimd, "g", tok_gm1row)
            p.inc(gpsimd.dma_start(out=cc1i.ap(), in_=row16[:]), "dP", 16)
            gpsimd.wait_ge(p.sems["dP"], p.cnt["dP"])
            p.lastw[("gp", "dP")] = p.cnt["dP"]
            p.inc(gpsimd.collective_compute(
                "AllReduce", ALU.max, replica_groups=[list(range(8))],
                ins=[cc1i.ap().opt()], outs=[cc1o.ap().opt()]), "cc")

        @block.scalar
        def _(scalar):
            p.wait(scalar, "s", ("cc", p.cnt["cc"]))
            p.inc(scalar.dma_start(out=row16[:], in_=cc1o.ap()), "dA", 16)
        tok_cc1 = p.tok("dA")

        @block.vector
        def _(vector):
            p.wait(vector, "v", tok_cc1)
            p.inc(vector.tensor_scalar(out=gm1[:], in0=row16[0:1, 0:1], scalar1=1e-6,
                                       scalar2=None, op0=ALU.max), "v")
            p.wait(vector, "v", p.tok("v"))
            p.inc(vector.tensor_scalar(out=sx_s[:], in0=gm1[:], scalar1=1.0 / 127, scalar2=None,
                                       op0=ALU.mult), "v")
            p.inc(vector.reciprocal(out=rsx[:], in_=gm1[:]), "v")
            p.inc(vector.tensor_scalar(out=s1_s[:], in0=wm1[:], scalar1=1.0 / 127,
                                       scalar2=None, op0=ALU.mult), "v")
            p.wait(vector, "v", p.tok("v"))
            p.inc(vector.tensor_scalar(out=rsx[:], in0=rsx[:], scalar1=127.0, scalar2=None,
                                       op0=ALU.mult), "v")
            p.inc(vector.tensor_tensor(out=s1_s[:], in0=s1_s[:], in1=sx_s[:], op=ALU.mult), "v")
        tok_scales1 = p.tok("v")

        @block.tensor
        def _(tensor):
            p.wait(tensor, "t", tok_scales1)
            p.inc(tensor.matmul(pb[2][:, 0:1], ones_f[:], rsx[:], start=True, stop=True), "t")
            p.inc(tensor.matmul(pb[2][:, 3:4], ones_f[:], s1_s[:], start=True, stop=True), "t")
        tok_bc1mm = p.tok("t")

        @block.vector
        def _(vector):
            p.wait(vector, "v", tok_bc1mm)
            p.inc(vector.tensor_copy(out=rsx_b[:], in_=pb[2][:, 0:1]), "v")
            p.inc(vector.tensor_copy(out=s1_b[:], in_=pb[2][:, 3:4]), "v")
            p.wait(vector, "v", p.tok("v"))
            p.inc(vector.tensor_tensor(out=b1s[:], in0=b1c[:],
                                       in1=s1_b[:, 0:1].broadcast_to((128, 32)), op=ALU.mult), "v")
        tok_bc1 = p.tok("v")

        if "scal" in dbg:
            @block.vector
            def _(vector):
                p.wait(vector, "v", tok_bc1)
                p.inc(vector.tensor_copy(out=row16[0:1, 0:1], in_=sx_s[:]), "v")
                p.inc(vector.tensor_copy(out=row16[0:1, 1:2], in_=wm1[:]), "v")
                p.inc(vector.tensor_copy(out=row16[0:1, 2:3], in_=wm2[:]), "v")
                p.inc(vector.tensor_copy(out=row16[0:1, 3:4], in_=s1_s[:]), "v")
            tks = p.tok("v")

            @block.sync
            def _(sync):
                p.wait(sync, "sy", tks)
                p.inc(sync.dma_start(out=dbg_outs["scal"], in_=row16[:]), "dS", 16)

        def quant_1024(vector, src, rs_col, dst_off):
            for ch in range(4):
                sl = slice(ch * 256, (ch + 1) * 256)
                p.inc(vector.tensor_scalar(out=sc_a[:, sl], in0=src[:, sl],
                                           scalar1=rs_col, scalar2=256.0,
                                           op0=ALU.mult, op1=ALU.add), "v")
                p.wait(vector, "v", p.tok("v"))
                p.inc(vector.tensor_scalar(out=sc_i[:], in0=sc_a[:, sl], scalar1=0.5,
                                           scalar2=None, op0=ALU.add), "v")
                p.wait(vector, "v", p.tok("v"))
                p.inc(vector.tensor_tensor(out=sc_i2[:], in0=sc_i[:], in1=sc_a[:, sl],
                                           op=ALU.subtract), "v")
                p.wait(vector, "v", p.tok("v"))
                p.inc(vector.scalar_tensor_tensor(out=AB[:, dst_off + ch * 256: dst_off + (ch + 1) * 256],
                                                  in0=sc_i[:], scalar=256.0, in1=sc_i2[:],
                                                  op0=ALU.subtract, op1=ALU.subtract), "v")
                p.wait(vector, "v", p.tok("v"))

        tok_w1ld, tok_w1q, tok_w1tr_t, tok_w1T, tok_w1T_lin = {}, {}, {}, {}, {}

        for r in range(32):
            srcbuf = wst[r % 2] if r % 2 == 0 else xob[0]

            @block.gpsimd
            def _(gpsimd, r=r):
                p.wait(gpsimd, "g", tok_w1q.get(r - 1))
                if r == 0:
                    p.wait(gpsimd, "g", ("cc", 1))
                if r < 2:
                    p.wait(gpsimd, "g", tok_wmax)
                    p.wait(gpsimd, "g", tok_p3_v)
                sem = "wst0P" if r % 2 == 0 else "xob0P"
                p.inc(gpsimd.dma_start(out=(wst[0] if r % 2 == 0 else xob[0])[:],
                                       in_=w1_in[r * 128:(r + 1) * 128, :]), sem, 16)
                tok_w1ld[r] = p.tok(sem)

            @block.vector
            def _(vector, r=r, srcbuf=srcbuf):
                p.wait(vector, "v", tok_w1ld[r])
                p.wait(vector, "v", tok_bcw)
                p.wait(vector, "v", tok_p2_v)
                p.wait(vector, "v", tok_w1tr_t.get(r - 1))
                quant_1024(vector, srcbuf, rsw1_b[:, 0:1], QSTG)
            tok_w1q[r] = p.tok("v")

            for c in range(8):
                @block.tensor
                def _(tensor, r=r, c=c):
                    p.wait(tensor, "t", tok_w1q[r])
                    if r == 0 and c == 0:
                        p.wait(tensor, "t", tok_p2_v)
                    p.wait(tensor, "t", tok_w1T_lin.get(r * 8 + c - 2))
                    p.inc(tensor.transpose(pbT[c % 2][:, 768:896],
                                           AB[:, QSTG + c * 128:QSTG + (c + 1) * 128], ident[:]), "t")
                tok_w1tr_t[(r, c)] = p.tok("t")

                @block.vector
                def _(vector, r=r, c=c):
                    p.wait(vector, "v", tok_w1tr_t[(r, c)])
                    p.inc(vector.tensor_copy(
                        out=AB[:, R1 + c * 4096 + r * 128: R1 + c * 4096 + (r + 1) * 128],
                        in_=pbT[c % 2][:, 768:896]), "v")
                tok_w1T[(r, c)] = p.tok("v")
                tok_w1T_lin[r * 8 + c] = p.tok("v")
            tok_w1tr_t[r] = p.tok("t")
        tok_w1 = p.tok("v")

        tok_xqv, tok_xqtr_t, tok_xqfm, tok_xqfm_lin = {}, {}, {}, {}
        for s_ in range(NQ):
            @block.sync
            def _(sync, s_=s_):
                p.wait(sync, "sy", tok_ydma[NQ - 2])
                p.wait(sync, "sy", tok_ydma[NQ - 1])
                p.wait(sync, "sy", tok_xqv.get(s_ - 1))
                p.inc(sync.dma_start(out=yb[s_ % 2][:], in_=y_d[s_ * 128:(s_ + 1) * 128, :]),
                      f"yb{s_ % 2}", 16)
            ytok = p.tok(f"yb{s_ % 2}")

            @block.vector
            def _(vector, s_=s_, ytok=ytok):
                p.wait(vector, "v", ytok)
                p.wait(vector, "v", tok_bc1)
                p.wait(vector, "v", tok_w1tr_t[31])
                p.wait(vector, "v", tok_xqtr_t.get(s_ - 1))
                quant_1024(vector, yb[s_ % 2], rsx_b[:, 0:1], QSTG)
            tok_xqv[s_] = p.tok("v")

            for c in range(8):
                @block.tensor
                def _(tensor, s_=s_, c=c):
                    p.wait(tensor, "t", tok_xqv[s_])
                    p.wait(tensor, "t", tok_xqfm_lin.get(s_ * 8 + c - 2))
                    p.inc(tensor.transpose(pbT[c % 2][:, 640:768],
                                           AB[:, QSTG + c * 128:QSTG + (c + 1) * 128], ident[:]), "t")
                tok_xqtr_t[(s_, c)] = p.tok("t")

                @block.vector
                def _(vector, s_=s_, c=c):
                    p.wait(vector, "v", tok_xqtr_t[(s_, c)])
                    if s_ == 0 and c == 0:
                        p.wait(vector, "v", tok_p2_v)
                    p.inc(vector.tensor_copy(
                        out=AB[:, R3 + c * 2048 + s_ * 128: R3 + c * 2048 + (s_ + 1) * 128],
                        in_=pbT[c % 2][:, 640:768]), "v")
                tok_xqfm[(s_, c)] = p.tok("v")
                tok_xqfm_lin[s_ * 8 + c] = p.tok("v")
            tok_xqtr_t[s_] = p.tok("t")
        tok_xq_all = p.tok("v")

        if "xqd" in dbg:
            @block.sync
            def _(sync):
                p.wait(sync, "sy", tok_xq_all)
                p.inc(sync.dma_start(out=dbg_outs["xqd"], in_=AB[:, R3:R3 + 16384]), "dS", 16)

        tok_hmm, tok_hv, tok_hmaxv, tok_hdma = {}, {}, {}, {}
        for rb in range(8):
            gelu_defer = []
            for ht in range(32):
                n = rb * 32 + ht
                bank = pb[n % 2]

                @block.tensor
                def _(tensor, rb=rb, ht=ht, n=n, bank=bank):
                    for c in range(8):
                        p.wait(tensor, "t", tok_w1T[(ht, c)])
                        p.wait(tensor, "t", tok_xqfm[(2 * rb, c)])
                        p.wait(tensor, "t", tok_xqfm[(2 * rb + 1, c)])
                    p.wait(tensor, "t", tok_hv.get(n - 2))
                    for c in range(8):
                        p.inc(tensor.matmul(
                            bank[:, 0:256],
                            AB[:, R1 + c * 4096 + ht * 128: R1 + c * 4096 + (ht + 1) * 128],
                            AB[:, R3 + c * 2048 + rb * 256: R3 + c * 2048 + (rb + 1) * 256],
                            start=(c == 0), stop=(c == 7)), "t")
                tok_hmm[n] = p.tok("t")

                tok_hv[n] = p.pre("s")
                gelu_defer.append((n, ht, bank))

                @block.vector
                def _(vector, n=n):
                    p.wait(vector, "v", tok_hv[n])
                    p.wait(vector, "v", p.tok("v"))
                    p.inc(vector.reduce_max(out=rmax[:], in_=hb[n % 2][:], axis=AX.X,
                                            apply_absolute_value=True), "v")
                    p.wait(vector, "v", p.tok("v"))
                    if n == 0:
                        p.inc(vector.tensor_copy(out=hmax[:], in_=rmax[:]), "v")
                    else:
                        p.inc(vector.tensor_tensor(out=hmax[:], in0=hmax[:], in1=rmax[:],
                                                   op=ALU.max), "v")
                tok_hmaxv[n] = p.tok("v")

                @block.gpsimd
                def _(gpsimd, rb=rb, ht=ht, n=n):
                    p.wait(gpsimd, "g", tok_hv[n])
                    p.inc(gpsimd.dma_start(out=h_d[ht * 128:(ht + 1) * 128,
                                                  rb * 256:(rb + 1) * 256],
                                           in_=hb[n % 2][:]), f"hb{n % 2}", 16)
                tok_hdma[n] = p.tok(f"hb{n % 2}")

            @block.scalar
            def _(scalar, gelu_defer=list(gelu_defer)):
                for n, ht, bank in gelu_defer:
                    p.wait(scalar, "s", tok_hmm[n])
                    p.wait(scalar, "s", tok_hmaxv.get(n - 2))
                    p.wait(scalar, "s", tok_hdma.get(n - 2))
                    p.emit_pre(scalar.activation(out=hb[n % 2][:], in_=bank[:, 0:256],
                                                 func=AF.Gelu, bias=b1s[:, ht:ht + 1],
                                                 scale=s1_b[:, 0:1]), "s")
        tok_ffn1_s = p.tok("s")
        tok_ffn1_max = p.tok("v")
        tok_hdma_last = (p.tok("hb0"), p.tok("hb1"))

        tok_w2ld, tok_w2q, tok_w2tr_t, tok_w2T, tok_w2T_lin = {}, {}, {}, {}, {}

        for i in range(32):
            r, hc = divmod(i, 4)
            srcbuf = wst[i % 2] if i % 2 == 0 else xob[1]

            @block.gpsimd
            def _(gpsimd, i=i, r=r, hc=hc):
                p.wait(gpsimd, "g", tok_w2q.get(i - 1))
                if i < 2:
                    p.wait(gpsimd, "g", tok_w1)
                sem = "wst0P" if i % 2 == 0 else "xob1P"
                p.inc(gpsimd.dma_start(out=(wst[0] if i % 2 == 0 else xob[1])[:],
                                       in_=w2_in[r * 128:(r + 1) * 128, hc * 1024:(hc + 1) * 1024]), sem, 16)
                tok_w2ld[i] = p.tok(sem)

            @block.vector
            def _(vector, i=i, srcbuf=srcbuf):
                p.wait(vector, "v", tok_w2ld[i])
                p.wait(vector, "v", tok_w1)
                p.wait(vector, "v", tok_w2tr_t.get(i - 1))
                quant_1024(vector, srcbuf, rsw2_b[:, 0:1], QSTG)
            tok_w2q[i] = p.tok("v")

            for c in range(8):
                @block.tensor
                def _(tensor, i=i, c=c):
                    p.wait(tensor, "t", tok_w2q[i])
                    if i == 0 and c == 0:
                        p.wait(tensor, "t", tok_p3_v)
                    p.wait(tensor, "t", tok_w2T_lin.get(i * 8 + c - 2))
                    p.inc(tensor.transpose(pbT[c % 2][:, 896:1024],
                                           AB[:, QSTG + c * 128:QSTG + (c + 1) * 128], ident[:]), "t")
                tok_w2tr_t[(i, c)] = p.tok("t")

                @block.vector
                def _(vector, i=i, c=c, r=r, hc=hc):
                    ht = hc * 8 + c
                    dst = R2 if ht < 16 else R4
                    hto = ht if ht < 16 else ht - 16
                    p.wait(vector, "v", tok_w2tr_t[(i, c)])
                    p.inc(vector.tensor_copy(
                        out=AB[:, dst + hto * 1024 + r * 128: dst + hto * 1024 + (r + 1) * 128],
                        in_=pbT[c % 2][:, 896:1024]), "v")
                tok_w2T[(i, c)] = p.tok("v")
                tok_w2T_lin[i * 8 + c] = p.tok("v")
            tok_w2tr_t[i] = p.tok("t")
        tok_w2 = p.tok("v")

        @block.tensor
        def _(tensor):
            p.wait(tensor, "t", tok_ffn1_max)
            p.inc(tensor.matmul(pb[2][0:1, 0:128], hmax[:], identf[:],
                                start=True, stop=True), "t")
        tok_hm_mm = p.tok("t")

        @block.vector
        def _(vector):
            p.wait(vector, "v", tok_hm_mm)
            p.inc(vector.reduce_max(out=gm2[:], in_=pb[2][0:1, 0:128], axis=AX.X), "v")
            p.wait(vector, "v", p.tok("v"))
            p.inc(vector.tensor_scalar(out=row16[:], in0=ones_f[0:1, 0:16],
                                       scalar1=gm2[0:1, 0:1], scalar2=None, op0=ALU.mult), "v")
        tok_gm2row = p.tok("v")

        @block.gpsimd
        def _(gpsimd):
            p.wait(gpsimd, "g", tok_gm2row)
            p.inc(gpsimd.dma_start(out=cc2i.ap(), in_=row16[:]), "dP", 16)
            gpsimd.wait_ge(p.sems["dP"], p.cnt["dP"])
            p.lastw[("gp", "dP")] = p.cnt["dP"]
            p.inc(gpsimd.collective_compute(
                "AllReduce", ALU.max, replica_groups=[list(range(8))],
                ins=[cc2i.ap().opt()], outs=[cc2o.ap().opt()]), "cc")

        @block.scalar
        def _(scalar):
            p.wait(scalar, "s", ("cc", p.cnt["cc"]))
            p.inc(scalar.dma_start(out=row16[:], in_=cc2o.ap()), "dA", 16)
        tok_cc2 = p.tok("dA")

        @block.vector
        def _(vector):
            p.wait(vector, "v", tok_cc2)
            p.inc(vector.tensor_scalar(out=gm2[:], in0=row16[0:1, 0:1], scalar1=1e-6,
                                       scalar2=None, op0=ALU.max), "v")
            p.wait(vector, "v", p.tok("v"))
            p.inc(vector.tensor_scalar(out=sh_s[:], in0=gm2[:], scalar1=1.0 / 127, scalar2=None,
                                       op0=ALU.mult), "v")
            p.inc(vector.reciprocal(out=rsh[:], in_=gm2[:]), "v")
            p.inc(vector.tensor_scalar(out=s2_s[:], in0=wm2[:], scalar1=1.0 / 127,
                                       scalar2=None, op0=ALU.mult), "v")
            p.wait(vector, "v", p.tok("v"))
            p.inc(vector.tensor_scalar(out=rsh[:], in0=rsh[:], scalar1=127.0, scalar2=None,
                                       op0=ALU.mult), "v")
            p.inc(vector.tensor_tensor(out=s2_s[:], in0=s2_s[:], in1=sh_s[:], op=ALU.mult), "v")
        tok_scales2 = p.tok("v")

        @block.tensor
        def _(tensor):
            p.wait(tensor, "t", tok_scales2)
            p.inc(tensor.matmul(pb[2][:, 4:5], ones_f[:], rsh[:], start=True, stop=True), "t")
            p.inc(tensor.matmul(pb[2][:, 5:6], ones_f[:], s2_s[:], start=True, stop=True), "t")
        tok_bc2mm = p.tok("t")

        @block.vector
        def _(vector):
            p.wait(vector, "v", tok_bc2mm)
            p.inc(vector.tensor_copy(out=rsh_b[:], in_=pb[2][:, 4:5]), "v")
            p.inc(vector.tensor_copy(out=s2_b[:], in_=pb[2][:, 5:6]), "v")
        tok_bc2 = p.tok("v")

        @block.sync
        def _(sync):
            p.wait(sync, "sy", tok_w2)
            p.inc(sync.dma_start(out=xob[0][0:1, :], in_=b2_in), "xob0", 16)
        b2ld = p.tok("xob0")

        @block.vector
        def _(vector):
            p.wait(vector, "v", b2ld)
            p.wait(vector, "v", tok_w2)
            p.inc(vector.tensor_copy(out=AB[0:1, QSTG:QSTG + 1024], in_=xob[0][0:1, :]), "v")
        tok_b2 = p.tok("v")

        tok_hq, tok_omm, tok_z5, tok_sq5, tok_outv, tok_outdma = {}, {}, {}, {}, {}, {}
        def emit_quants(rb):
            hqo = HQ0 if rb % 2 == 0 else HQ1
            for ht in range(32):
                n = rb * 32 + ht

                hbufs = [hb[0][:], hb[1][:], sc_a[:, 0:256], sc_a[:, 256:512]]
                hsem = ["hb0", "hb1", "hb2", "hb3"][n % 4]

                @block.gpsimd
                def _(gpsimd, rb=rb, ht=ht, n=n, hsem=hsem):
                    p.wait(gpsimd, "g", tok_hq.get(n - 3))
                    p.wait(gpsimd, "g", tok_omm.get(rb - 2))
                    if n == 0:
                        p.wait(gpsimd, "g", tok_hdma_last[0])
                        p.wait(gpsimd, "g", tok_hdma_last[1])
                        p.wait(gpsimd, "g", tok_ffn1_s)
                        p.wait(gpsimd, "g", tok_w1)
                    if n in (2, 3):
                        p.wait(gpsimd, "g", tok_w2)
                    p.inc(gpsimd.dma_start(out=hbufs[n % 4],
                                           in_=h_d[ht * 128:(ht + 1) * 128,
                                                   rb * 256:(rb + 1) * 256]), hsem, 16)
                htok = p.tok(hsem)

                @block.vector
                def _(vector, ht=ht, n=n, htok=htok, hqo=hqo, hbufs=hbufs):
                    p.wait(vector, "v", htok)
                    p.wait(vector, "v", tok_bc2)
                    if n == 0:
                        p.wait(vector, "v", tok_w2)
                    p.inc(vector.tensor_scalar(out=wst[0][:, 0:256], in0=hbufs[n % 4],
                                               scalar1=rsh_b[:, 0:1], scalar2=256.0,
                                               op0=ALU.mult, op1=ALU.add), "v")
                    p.wait(vector, "v", p.tok("v"))
                    p.inc(vector.tensor_scalar(out=sc_i[:, 0:256], in0=wst[0][:, 0:256], scalar1=0.5,
                                               scalar2=None, op0=ALU.add), "v")
                    p.wait(vector, "v", p.tok("v"))
                    p.inc(vector.tensor_tensor(out=sc_i2[:, 0:256], in0=sc_i[:, 0:256], in1=wst[0][:, 0:256],
                                               op=ALU.subtract), "v")
                    p.wait(vector, "v", p.tok("v"))
                    p.inc(vector.scalar_tensor_tensor(
                        out=AB[:, hqo + ht * 256: hqo + (ht + 1) * 256], in0=sc_i[:, 0:256],
                        scalar=256.0, in1=sc_i2[:, 0:256], op0=ALU.subtract, op1=ALU.subtract), "v")
                tok_hq[n] = p.tok("v")

        def emit_rb_mm(rb):
            hqo = HQ0 if rb % 2 == 0 else HQ1

            @block.sync
            def _(sync, rb=rb):
                p.wait(sync, "sy", tok_xq_all)
                p.wait(sync, "sy", tok_z5.get((rb - 1, 1)))
                p.inc(sync.dma_start(out=yb[0][:], in_=y_d[rb * 256:rb * 256 + 128, :]), "yb0", 16)
                p.inc(sync.dma_start(out=yb[1][:], in_=y_d[rb * 256 + 128:(rb + 1) * 256, :]), "yb1", 16)
            ytok5a, ytok5b = p.tok("yb0"), p.tok("yb1")

            @block.tensor
            def _(tensor, rb=rb, hqo=hqo):
                p.wait(tensor, "t", tok_w2)
                if rb == 0:
                    p.wait(tensor, "t", tok_bc2)
                    p.wait(tensor, "t", tok_b2)
                p.wait(tensor, "t", tok_z5.get((rb - 1, 1)))
                for ht in range(32):
                    p.wait(tensor, "t", tok_hq[rb * 32 + ht])
                for rt in range(2):
                    for chn in range(2):
                        bank = pb[2 + 2 * rt + chn]
                        for ht in range(32):
                            dst = R2 if ht < 16 else R4
                            hto = ht if ht < 16 else ht - 16
                            p.inc(tensor.matmul(
                                bank[:, 0:512],
                                AB[:, hqo + ht * 256 + rt * 128: hqo + ht * 256 + rt * 128 + 128],
                                AB[:, dst + hto * 1024 + chn * 512: dst + hto * 1024 + (chn + 1) * 512],
                                start=(ht == 0), stop=False), "t")
                        p.inc(tensor.matmul(bank[:, 0:512], ones_b[:],
                                            AB[0:1, QSTG + chn * 512:QSTG + (chn + 1) * 512],
                                            start=False, stop=True), "t")
            tok_omm[rb] = p.tok("t")
            return ytok5a, ytok5b

        def emit_rb_tail(rb, ytok5a, ytok5b):
            for rt in range(2):
                s_ = 2 * rb + rt

                @block.vector
                def _(vector, rb=rb, rt=rt, ytok5a=ytok5a, ytok5b=ytok5b):
                    p.wait(vector, "v", ytok5a)
                    p.wait(vector, "v", ytok5b)
                    p.wait(vector, "v", tok_omm[rb])
                    p.wait(vector, "v", tok_sq5.get((rb, rt - 1)))
                    p.wait(vector, "v", tok_sq5.get((rb - 1, 1)))
                    p.wait(vector, "v", tok_outv.get((rb, rt - 1)))
                    p.wait(vector, "v", tok_outv.get((rb - 1, 1)))
                    for chn in range(2):
                        bank = pb[2 + 2 * rt + chn]
                        p.inc(vector.scalar_tensor_tensor(
                            out=zb[:, chn * 512:(chn + 1) * 512], in0=bank[:, 0:512],
                            scalar=s2_b[:, 0:1], in1=yb[rt][:, chn * 512:(chn + 1) * 512],
                            op0=ALU.mult, op1=ALU.add), "v")
                tok_z5[(rb, rt)] = p.tok("v")

                @block.vector
                def _(vector, rb=rb, rt=rt):
                    p.wait(vector, "v", tok_z5[(rb, rt)])
                    p.inc(vector.scalar_tensor_tensor(out=wst[1][:], in0=zb[:], scalar=1.0,
                                                      in1=zb[:], op0=ALU.mult, op1=ALU.mult,
                                                      accum_out=s2c[:]), "v")
                tok_sq5[(rb, rt)] = p.tok("v")

                @block.vector
                def _(vector, rb=rb, rt=rt):
                    p.wait(vector, "v", tok_z5[(rb, rt)])
                    p.inc(vector.reduce_sum(out=sum1[:], in_=zb[:], axis=AX.X), "v")
                    p.wait(vector, "v", tok_sq5[(rb, rt)])
                    p.wait(vector, "v", p.tok("v"))
                    p.inc(vector.tensor_scalar(out=mu1[:], in0=sum1[:], scalar1=1.0 / D,
                                               scalar2=None, op0=ALU.mult), "v")
                    p.wait(vector, "v", p.tok("v"))
                    p.inc(vector.scalar_tensor_tensor(out=tmp1[:], in0=mu1[:], scalar=1.0,
                                                      in1=mu1[:], op0=ALU.mult, op1=ALU.mult), "v")
                    p.inc(vector.tensor_scalar(out=var1[:], in0=s2c[:], scalar1=1.0 / D,
                                               scalar2=None, op0=ALU.mult), "v")
                    p.wait(vector, "v", p.tok("v"))
                    p.inc(vector.tensor_tensor(out=var1[:], in0=var1[:], in1=tmp1[:],
                                               op=ALU.subtract), "v")
                tok_var5 = p.tok("v")

                @block.scalar
                def _(scalar, tok_var5=tok_var5):
                    p.wait(scalar, "s", tok_var5)
                    p.inc(scalar.activation(out=sd1[:], in_=var1[:], func=AF.Sqrt,
                                            bias=eps_c[:, 0:1]), "s")
                tok_sd5 = p.tok("s")

                @block.vector
                def _(vector, rt=rt, s_=s_, tok_sd5=tok_sd5):
                    p.wait(vector, "v", tok_sd5)
                    p.wait(vector, "v", tok_outdma.get(s_ - 2))
                    p.wait(vector, "v", tok_w2)
                    p.inc(vector.reciprocal(out=rstd1[:], in_=sd1[:]), "v")
                    p.wait(vector, "v", p.tok("v"))
                    p.inc(vector.tensor_scalar(out=xob[rt][:], in0=zb[:], scalar1=mu1[:, 0:1],
                                               scalar2=rstd1[:, 0:1], op0=ALU.subtract,
                                               op1=ALU.mult), "v")
                tok_fin = p.tok("v")
                tok_outv[(rb, rt)] = tok_fin

                @block.gpsimd
                def _(gpsimd, s_=s_, rt=rt, tok_fin=tok_fin):
                    p.wait(gpsimd, "g", tok_fin)
                    p.inc(gpsimd.dma_start(out=out_d[s_ * 128:(s_ + 1) * 128, :],
                                           in_=xob[rt][:]), f"xob{rt}P", 16)
                tok_outdma[s_] = p.tok(f"xob{rt}P")

        emit_quants(0)
        for rb in range(8):
            yt = emit_rb_mm(rb)
            if rb < 7:
                emit_quants(rb + 1)
            emit_rb_tail(rb, *yt)

        @block.sync
        def _(sync):
            for sname in ["dP", "dS", "dA", "wst0", "wst1", "xob0", "xob1",
                          "yb0", "yb1", "hb0", "hb1",
                          "wst0P", "wst1P", "xob0P", "xob1P"]:
                if p.cnt[sname]:
                    sync.wait_ge(p.sems[sname], p.cnt[sname])

        @block.gpsimd
        def _(gpsimd):
            gpsimd.wait_ge(p.sems["xob0P"], p.cnt["xob0P"])
            gpsimd.wait_ge(p.sems["xob1P"], p.cnt["xob1P"])

    return nc


_NC_CACHE = {}


def kernel(L0_attn_x, L0_attn_Wq, L0_attn_Wkv, L0_attn_Wo, L0_attn_bo,
           L0_attn_ln_g, L0_attn_ln_b, L0_ffn_w1, L0_ffn_b1, L0_ffn_w2,
           L0_ffn_b2, dbg=DBG, _ret_raw=False):
    x = np.ascontiguousarray(np.asarray(L0_attn_x, dtype=np.float32))
    B = x.shape[0]
    consts = dict(
        Wq=np.ascontiguousarray(np.asarray(L0_attn_Wq, np.float32)),
        Wkv=np.ascontiguousarray(np.asarray(L0_attn_Wkv, np.float32)),
        Wo=np.ascontiguousarray(np.asarray(L0_attn_Wo, np.float32)),
        bo=np.asarray(L0_attn_bo, np.float32).reshape(1, D),
        g_rep=np.ascontiguousarray(np.broadcast_to(
            np.tile(np.asarray(L0_attn_ln_g, np.float32), H)[None, :], (128, D))),
        b_rep=np.ascontiguousarray(np.broadcast_to(
            np.tile(np.asarray(L0_attn_ln_b, np.float32), H)[None, :], (128, D))),
        w1=np.ascontiguousarray(np.asarray(L0_ffn_w1, np.float32)),
        b1c=np.ascontiguousarray(np.asarray(L0_ffn_b1, np.float32).reshape(32, 128).T),
        w2=np.ascontiguousarray(np.asarray(L0_ffn_w2, np.float32)),
        b2=np.asarray(L0_ffn_b2, np.float32).reshape(1, D),
        ident=np.eye(128, dtype=ml_dtypes.bfloat16),
        iota_r=np.arange(128, dtype=np.float32).reshape(1, 128),
        ones_f=np.ones((1, 128), np.float32),
    )
    ii = np.arange(128)
    tri = (ii[:, None] <= ii[None, :]).astype(ml_dtypes.bfloat16)
    onesm = np.ones((128, 128), ml_dtypes.bfloat16)
    zerosm = np.zeros((128, 128), ml_dtypes.bfloat16)

    key = tuple(sorted(dbg))
    if key not in _NC_CACHE:
        _NC_CACHE[key] = build(dbg)
    nc = _NC_CACHE[key]

    in_maps, metas = [], []
    for c in range(8):
        b, h = divmod(c, 2)
        out_tiles = [2 * s + h for s in range(NQ)]
        orig_tiles = [t + 12 if t < 20 else t - 20 for t in out_tiles]
        xb = x[b]
        x_q = np.concatenate([xb[t * 128:(t + 1) * 128] for t in orig_tiles], axis=0)
        x_res = np.concatenate([xb[t * 128:(t + 1) * 128] for t in out_tiles], axis=0)
        pos_q = np.concatenate([np.arange(t * 128, (t + 1) * 128) for t in orig_tiles]) \
            .astype(np.float32).reshape(1, -1)
        m = dict(consts)
        m.update(x_b=xb, x_q=np.ascontiguousarray(x_q),
                 x_res=np.ascontiguousarray(x_res), pos_q=pos_q,
                 mask_a=(tri if h == 0 else onesm), mask_b=(zerosm if h == 0 else tri))
        in_maps.append(m)
        metas.append((b, out_tiles))

    res = run_bass_kernel_spmd(nc, in_maps, core_ids=list(range(8)))
    if _ret_raw:
        return res, metas

    out = np.empty((B, SEQ, D), dtype=np.float32)
    for c in range(8):
        b, out_tiles = metas[c]
        o = np.asarray(res.results[c]["out"]).reshape(NQ, 128, D)
        for s, t in enumerate(out_tiles):
            out[b, t * 128:(t + 1) * 128] = o[s]
    return out


# version 21 (final)
# speedup vs baseline: 1.3798x; 1.0025x over previous
import numpy as np
import ml_dtypes
import concourse.bass as bass
import concourse.mybir as mybir
from concourse.bass_utils import run_bass_kernel_spmd

F32 = mybir.dt.float32
BF16 = mybir.dt.bfloat16
I32 = mybir.dt.int32
AF = mybir.ActivationFunctionType
ALU = mybir.AluOpType
AX = mybir.AxisListType

SEQ, D, H, DH = 4096, 1024, 16, 64
NT = SEQ // 128
NQ = 16
NLAT = 10
CTXT = 12
EPS = 1e-5
DBG = ()


def slot_kts(s):
    if s < NLAT:
        return list(range(12 + 2 * s + 2))
    return list(range(CTXT))


class P:
    def __init__(self, nc):
        self.nc = nc
        self.sems = {}
        self.cnt = {}
        self.lastw = {}

    def mk(self, es, name):
        self.sems[name] = es.enter_context(self.nc.semaphore(name))
        self.cnt[name] = 0

    def inc(self, ins, name, step=1):
        ins.then_inc(self.sems[name], step)
        self.cnt[name] += step
        return (name, self.cnt[name])

    def tok(self, name):
        return (name, self.cnt[name])

    def pre(self, name, step=1):
        self.cnt[name] += step
        return (name, self.cnt[name])

    def emit_pre(self, ins, name, step=1):
        ins.then_inc(self.sems[name], step)

    def wait(self, eng, ekey, tok):
        if tok is None:
            return
        name, val = tok
        if val <= 0:
            return
        k = (ekey, name)
        if self.lastw.get(k, -1) >= val:
            return
        eng.wait_ge(self.sems[name], val)
        self.lastw[k] = val


NONE_T = None


def build(dbg=DBG):
    nc = bass.Bass("TRN2", target_bir_lowering=False, debug=False)

    x_in = nc.dram_tensor("x_b", [SEQ, D], F32, kind="ExternalInput").ap()
    xq_in = nc.dram_tensor("x_q", [NQ * 128, D], F32, kind="ExternalInput").ap()
    xr_in = nc.dram_tensor("x_res", [NQ * 128, D], F32, kind="ExternalInput").ap()
    pos_in = nc.dram_tensor("pos_q", [1, NQ * 128], F32, kind="ExternalInput").ap()
    wq_in = nc.dram_tensor("Wq", [D, D], F32, kind="ExternalInput").ap()
    wkv_in = nc.dram_tensor("Wkv", [D, D], F32, kind="ExternalInput").ap()
    wo_in = nc.dram_tensor("Wo", [D, D], F32, kind="ExternalInput").ap()
    bo_in = nc.dram_tensor("bo", [1, D], F32, kind="ExternalInput").ap()
    lng_in = nc.dram_tensor("g_rep", [128, D], F32, kind="ExternalInput").ap()
    lnb_in = nc.dram_tensor("b_rep", [128, D], F32, kind="ExternalInput").ap()
    w1_in = nc.dram_tensor("w1", [4 * D, D], F32, kind="ExternalInput").ap()
    b1_in = nc.dram_tensor("b1c", [128, 32], F32, kind="ExternalInput").ap()
    w2_in = nc.dram_tensor("w2", [D, 4 * D], F32, kind="ExternalInput").ap()
    b2_in = nc.dram_tensor("b2", [1, D], F32, kind="ExternalInput").ap()
    id_in = nc.dram_tensor("ident", [128, 128], BF16, kind="ExternalInput").ap()
    ma_in = nc.dram_tensor("mask_a", [128, 128], BF16, kind="ExternalInput").ap()
    mb_in = nc.dram_tensor("mask_b", [128, 128], BF16, kind="ExternalInput").ap()
    iota_in = nc.dram_tensor("iota_r", [1, 128], F32, kind="ExternalInput").ap()
    onesf_in = nc.dram_tensor("ones_f", [1, 128], F32, kind="ExternalInput").ap()

    out_d = nc.dram_tensor("out", [NQ * 128, D], F32, kind="ExternalOutput").ap()
    y_d = nc.dram_tensor("y_sp", [NQ * 128, D], F32,
                         kind="ExternalOutput" if "y" in dbg else "Internal").ap()
    h_d = nc.dram_tensor("h_sp", [4 * D, NQ * 128], F32,
                         kind="ExternalOutput" if "h" in dbg else "Internal").ap()
    cc1i = nc.dram_tensor("cc1i", [1, 16], F32)
    cc1o = nc.dram_tensor("cc1o", [1, 16], F32)
    cc2i = nc.dram_tensor("cc2i", [1, 16], F32)
    cc2o = nc.dram_tensor("cc2o", [1, 16], F32)
    dbg_outs = {}
    for nm, shape, dt in [
        ("lkv", [128, NT * 1024], BF16), ("qfm", [128, NQ * 1024], BF16),
        ("avfm", [128, 16384], BF16), ("xqd", [128, 16384], BF16),
        ("scal", [1, 16], F32),
    ]:
        if nm in dbg:
            dbg_outs[nm] = nc.dram_tensor("dbg_" + nm, shape, dt, kind="ExternalOutput").ap()

    import contextlib
    es = contextlib.ExitStack()
    sb = lambda n, s, d: es.enter_context(nc.sbuf_tensor(n, s, d))

    AB = sb("AB", [128, 83456], BF16)
    R1, R2, R3, R4, QSTG = 0, 33280, 49664, 66048, 82432
    LKVS = 1040
    XROT, QXROT, PSB, LKVF, WOT = R4, R4 + 2048, R4 + 3072, R4 + 4096, R4 + 8192
    WQT, WKVT = R3, R3 + 8192
    HQ0, HQ1 = R1, R1 + 8192

    ctb = [sb("ctb0", [128, 32], F32)] * 2
    stb = [sb("stb0", [128, 32], F32)] * 2
    ntb = [sb("ntb0", [128, 32], F32)] * 2
    sc_a = sb("sc_a", [128, 1024], F32)
    sc_i = sb("sc_i", [128, 256], I32)
    sc_i2 = sb("sc_i2", [128, 256], I32)
    zb = sb("zb", [128, 1024], F32)
    yb = [sb(f"yb{i}", [128, 1024], F32) for i in range(2)]
    xob = [sb(f"xob{i}", [128, 1024], F32) for i in range(2)]
    wst = [sb(f"wst{i}", [128, 1024], F32) for i in range(2)]
    hb = [sb(f"hb{i}", [128, 256], F32) for i in range(2)]
    ident = sb("identb", [128, 128], BF16)
    identf = sb("identf", [128, 128], F32)
    mask_a = sb("mask_ab", [128, 128], BF16)
    mask_b = sb("mask_bb", [128, 128], BF16)
    iota_r = sb("iota_rb", [1, 128], F32)
    ones_f = sb("ones_fb", [1, 128], F32)
    ones_b = sb("ones_bb", [1, 128], BF16)
    onec_b = sb("onec_bb", [128, 1], BF16)
    b1c = sb("b1cb", [128, 32], F32)
    b1s = sb("b1sb", [128, 32], F32)
    gbb = sb("gbb", [128, D], BF16)
    bbb = sb("bbb", [128, D], BF16)
    eps_c = sb("eps_c", [128, 1], F32)
    invf = sb("invf", [1, 32], F32)
    c1 = sb("c1", [1, 1], F32)
    c2 = sb("c2", [1, 1], F32)
    twopi = sb("twopi", [1, 1], F32)
    r2pi = sb("r2pi", [1, 1], F32)

    r2pi_b = sb("r2pi_b", [128, 1], F32)
    twopi_b = sb("twopi_b", [128, 1], F32)
    mu8 = sb("mu8", [128, 8], F32)
    s28 = sb("s28", [128, 8], F32)
    var8 = sb("var8", [128, 8], F32)
    sd8 = sb("sd8", [128, 8], F32)
    rstd8 = sb("rstd8", [128, 8], F32)
    rec_q = sb("rec_q", [128, 1], F32)
    sum1 = sb("sum1", [128, 1], F32)
    s2c = sb("s2c", [128, 1], F32)
    mu1 = sb("mu1", [128, 1], F32)
    var1 = sb("var1", [128, 1], F32)
    sd1 = sb("sd1", [128, 1], F32)
    rstd1 = sb("rstd1", [128, 1], F32)
    tmp1 = sb("tmp1", [128, 1], F32)
    ymax = sb("ymax", [128, 1], F32)
    hmax = sb("hmax", [128, 1], F32)
    wmax1 = sb("wmax1", [128, 1], F32)
    wmax2 = sb("wmax2", [128, 1], F32)
    rmax = sb("rmax", [128, 1], F32)
    row16 = sb("row16", [1, 16], F32)
    gm1 = sb("gm1", [1, 1], F32)
    gm2 = sb("gm2", [1, 1], F32)
    wm1 = sb("wm1", [1, 1], F32)
    wm2 = sb("wm2", [1, 1], F32)
    sx_s = sb("sx_s", [1, 1], F32)
    sh_s = sb("sh_s", [1, 1], F32)
    s1_s = sb("s1_s", [1, 1], F32)
    s2_s = sb("s2_s", [1, 1], F32)
    rsx = sb("rsx", [1, 1], F32)
    rsh = sb("rsh", [1, 1], F32)
    rsw1 = sb("rsw1", [1, 1], F32)
    rsw2 = sb("rsw2", [1, 1], F32)
    rsx_b = sb("rsx_b", [128, 1], F32)
    rsh_b = sb("rsh_b", [128, 1], F32)
    rsw1_b = sb("rsw1_b", [128, 1], F32)
    rsw2_b = sb("rsw2_b", [128, 1], F32)
    s1_b = sb("s1_b", [128, 1], F32)
    s2_b = sb("s2_b", [128, 1], F32)

    pb = [es.enter_context(nc.psum_tensor(f"pb{i}", [128, 512], F32)) for i in range(6)]
    pbT = [es.enter_context(nc.psum_tensor(f"pbT{i}", [128, 1024], BF16)) for i in range(2)]

    p = P(nc)
    for s in ["dS", "dP", "dA", "v", "t", "s", "g", "cc",
              "wst0", "wst1", "xob0", "xob1", "yb0", "yb1", "hb0", "hb1",
              "wst0P", "wst1P", "xob0P", "xob1P", "hb2", "hb3"]:
        p.mk(es, s)

    with es, nc.Block() as block:
        @block.sync
        def _(sync):
            for dst, src in [(ident, id_in), (mask_a, ma_in), (mask_b, mb_in),
                             (iota_r, iota_in), (ones_f, onesf_in),
                             (b1c, b1_in)]:
                p.inc(sync.dma_start(out=dst[:], in_=src), "dS", 16)
        tok_ld0 = p.tok("dS")

        @block.vector
        def _(vector):
            p.wait(vector, "v", tok_ld0)

            p.inc(vector.memset(ones_b[:], 1.0), "v")
            p.inc(vector.memset(onec_b[:], 1.0), "v")
            p.inc(vector.memset(eps_c[:], EPS), "v")
            p.inc(vector.memset(c1[:], 10000.0), "v")
            p.inc(vector.memset(c2[:], 1.0), "v")
            p.inc(vector.tensor_copy(out=identf[:], in_=ident[:]), "v")
        tok_consts_v = p.tok("v")
        tok_idf = tok_consts_v

        @block.scalar
        def _(scalar):
            p.wait(scalar, "s", tok_consts_v)
            p.inc(scalar.activation(out=c1[:], in_=c1[:], func=AF.Ln), "s")
            p.inc(scalar.activation(out=c2[:], in_=c2[:], func=AF.Arctan), "s")
        tok_lnatan = p.tok("s")

        @block.vector
        def _(vector):
            p.wait(vector, "v", tok_lnatan)
            p.inc(vector.tensor_scalar(out=twopi[:], in0=c2[:], scalar1=8.0, scalar2=None,
                                       op0=ALU.mult), "v")
            p.wait(vector, "v", p.tok("v"))
            p.inc(vector.reciprocal(out=r2pi[:], in_=twopi[:]), "v")
            p.inc(vector.tensor_scalar(out=invf[:], in0=iota_r[0:1, 0:32],
                                       scalar1=c1[0:1, 0:1], scalar2=-1.0 / 32.0,
                                       op0=ALU.mult, op1=ALU.mult), "v")
        tok_invf_pre = p.tok("v")

        @block.scalar
        def _(scalar):
            p.wait(scalar, "s", tok_invf_pre)
            p.inc(scalar.activation(out=invf[:], in_=invf[:], func=AF.Exp), "s")
        tok_invf = p.tok("s")

        @block.sync
        def _(sync):
            p.wait(sync, "sy", tok_consts_v)
            p.inc(sync.dma_start(out=wst[0][:], in_=lng_in), "dS", 16)
            p.inc(sync.dma_start(out=wst[1][:], in_=lnb_in), "dS", 16)
        tok_gbld = p.tok("dS")

        @block.tensor
        def _(tensor):
            p.wait(tensor, "t", tok_invf)
            p.inc(tensor.matmul(pb[5][:, 0:1], ones_f[:], r2pi[:], start=True, stop=True), "t")
            p.inc(tensor.matmul(pb[5][:, 1:2], ones_f[:], twopi[:], start=True, stop=True), "t")
        tok_bc_mm = p.tok("t")

        @block.vector
        def _(vector):
            p.wait(vector, "v", tok_bc_mm)
            p.wait(vector, "v", tok_gbld)
            p.inc(vector.memset(
                AB[:, R1:R1 + NT * LKVS].rearrange("p (g d) -> p g d", d=65)[:, :, 64:65], 1.0), "v")
            p.inc(vector.tensor_copy(out=r2pi_b[:], in_=pb[5][:, 0:1]), "v")
            p.inc(vector.tensor_copy(out=twopi_b[:], in_=pb[5][:, 1:2]), "v")
            p.inc(vector.tensor_copy(out=gbb[:], in_=wst[0][:]), "v")
            p.inc(vector.tensor_copy(out=bbb[:], in_=wst[1][:]), "v")
        tok_bc = p.tok("v")

        tok_tbl_mm = {}
        tok_tbl_sin = {}

        def emit_table_tile(kind, tt, li):
            if kind == "kv":
                @block.vector
                def _(vector, tt=tt):
                    p.wait(vector, "v", tok_bc)
                    p.wait(vector, "v", p.tok("t"))
                    p.inc(vector.tensor_scalar(out=yb[0][0:1, 64:192], in0=iota_r[:],
                                               scalar1=float(128 * tt), scalar2=None,
                                               op0=ALU.add), "v")
                trow_tok = p.tok("v")
            else:
                @block.scalar
                def _(scalar, tt=tt, li=li):
                    p.wait(scalar, "s", tok_tbl_cons.get(li - 1))
                    p.inc(scalar.dma_start(out=yb[0][0:1, 64:192],
                                           in_=pos_in[0:1, tt * 128:(tt + 1) * 128]), "dA", 16)
                trow_tok = p.tok("dA")

            @block.tensor
            def _(tensor, trow_tok=trow_tok):
                p.wait(tensor, "tq", trow_tok)
                p.wait(tensor, "tq", tok_bc)
                p.wait(tensor, "tq", tok_tbl_cons.get(li - 1))
                p.inc(tensor.matmul(pb[5][:, 64:96], yb[0][0:1, 64:192], invf[:],
                                    start=True, stop=True), "t")
            mm_tok = p.tok("t")
            tok_tbl_mm[li] = mm_tok

            for off, dstb in ((0.25, ctb[li % 2]), (0.0, stb[li % 2])):
                prev_sin = p.tok("s")

                @block.vector
                def _(vector, off=off, mm_tok=mm_tok, li=li, prev_sin=prev_sin):
                    p.wait(vector, "v", mm_tok)
                    p.wait(vector, "v", prev_sin)
                    p.wait(vector, "v", tok_tbl_free.get(li - 1))
                    p.inc(vector.tensor_scalar(out=yb[0][:, 0:32], in0=pb[5][:, 64:96],
                                               scalar1=r2pi_b[:, 0:1], scalar2=off,
                                               op0=ALU.mult, op1=ALU.add), "v")
                    p.wait(vector, "v", p.tok("v"))
                    p.inc(vector.tensor_scalar(out=sc_i[:, 0:32], in0=yb[0][:, 0:32], scalar1=0.5,
                                               scalar2=None, op0=ALU.add), "v")
                    p.wait(vector, "v", p.tok("v"))
                    p.inc(vector.tensor_tensor(out=sc_i2[:, 0:32], in0=sc_i[:, 0:32],
                                               in1=yb[0][:, 0:32], op=ALU.subtract), "v")
                    p.wait(vector, "v", p.tok("v"))
                    p.inc(vector.scalar_tensor_tensor(out=yb[0][:, 32:64], in0=sc_i[:, 0:32],
                                                      scalar=-1.0, in1=yb[0][:, 0:32],
                                                      op0=ALU.mult, op1=ALU.add), "v")
                    p.wait(vector, "v", p.tok("v"))
                    p.inc(vector.tensor_tensor(out=yb[0][:, 32:64], in0=yb[0][:, 32:64], in1=sc_i2[:, 0:32],
                                               op=ALU.add), "v")
                    p.wait(vector, "v", p.tok("v"))
                    p.inc(vector.tensor_scalar(out=yb[0][:, 32:64], in0=yb[0][:, 32:64],
                                               scalar1=twopi_b[:, 0:1], scalar2=None,
                                               op0=ALU.mult), "v")
                arg_tok = p.tok("v")

                @block.scalar
                def _(scalar, dstb=dstb, arg_tok=arg_tok):
                    p.wait(scalar, "s", arg_tok)
                    p.inc(scalar.activation(out=dstb[:], in_=yb[0][:, 32:64], func=AF.Sin), "s")
            sin_done = p.tok("s")

            @block.vector
            def _(vector, li=li, sin_done=sin_done):
                p.wait(vector, "v", sin_done)
                p.inc(vector.tensor_scalar(out=ntb[li % 2][:], in0=stb[li % 2][:],
                                           scalar1=-1.0, scalar2=None, op0=ALU.mult), "v")
            tok_tbl_sin[li] = p.tok("v")
            tok_tbl_cons[li] = p.tok("v")

        tok_wcast, tok_wtr_t, tok_wevac = {}, {}, {}
        wlin = [0]
        for w_in, dstoff, scale in ((wq_in, WQT, 0.125), (wkv_in, WKVT, 1.0),
                                    (wo_in, WOT, 1.0)):
            for r in range(8):
                li = wlin[0]; wlin[0] += 1
                buf = wst[li % 2]

                @block.sync
                def _(sync, li=li, buf=buf, w_in=w_in, r=r):
                    p.wait(sync, "sy", tok_wcast.get(li - 1))
                    if li < 2:
                        p.wait(sync, "sy", tok_bc)
                    p.inc(sync.dma_start(out=buf[:], in_=w_in[r * 128:(r + 1) * 128, :]),
                          f"wst{li % 2}", 16)
                ltok = p.tok(f"wst{li % 2}")

                @block.vector
                def _(vector, li=li, buf=buf, ltok=ltok, scale=scale):
                    p.wait(vector, "v", ltok)
                    p.wait(vector, "v", tok_wtr_t.get(li - 1))
                    if scale != 1.0:
                        p.inc(vector.tensor_scalar(out=AB[:, QXROT:QXROT + 1024], in0=buf[:],
                                                   scalar1=scale, scalar2=None, op0=ALU.mult), "v")
                    else:
                        p.inc(vector.tensor_copy(out=AB[:, QXROT:QXROT + 1024], in_=buf[:]), "v")
                tok_wcast[li] = p.tok("v")
                ctok = p.tok("v")

                for c in range(8):
                    @block.tensor
                    def _(tensor, li=li, c=c, ctok=ctok):
                        p.wait(tensor, "t", ctok)
                        p.wait(tensor, "t", tok_wevac.get(li * 8 + c - 2))
                        p.inc(tensor.transpose(pbT[c % 2][:, 0:128],
                                               AB[:, QXROT + c * 128:QXROT + (c + 1) * 128],
                                               ident[:]), "t")
                    tok_wtr_t[(li, c)] = p.tok("t")

                    @block.vector
                    def _(vector, li=li, c=c, r=r, dstoff=dstoff):
                        p.wait(vector, "v", tok_wtr_t[(li, c)])
                        p.inc(vector.tensor_copy(
                            out=AB[:, dstoff + c * 1024 + r * 128: dstoff + c * 1024 + (r + 1) * 128],
                            in_=pbT[c % 2][:, 0:128]), "v")
                    tok_wevac[li * 8 + c] = p.tok("v")
                tok_wtr_t[li] = p.tok("t")
        tok_weights = p.tok("v")

        tok_xcast, tok_xtr_t, tok_xfm_ev, tok_projmm = {}, {}, {}, {}
        tok_rope_psum_free, tok_lkv, tok_qtr_t, tok_qfm = {}, {}, {}, {}
        tok_rot_cons = {}
        tok_qfm_lin = {}
        tok_tbl_cons = {}
        tok_tbl_free = {}
        plin = [0]

        def emit_proj_tile(kind, tt):
            li = plin[0]; plin[0] += 1
            key = (kind, tt)
            emit_table_tile(kind, tt, li)
            buf = wst[li % 2]
            src = x_in[tt * 128:(tt + 1) * 128, :] if kind == "kv" else \
                xq_in[tt * 128:(tt + 1) * 128, :]
            xroto = QSTG

            @block.sync
            def _(sync, li=li, buf=buf, src=src):
                p.wait(sync, "sy", tok_xcast.get(li - 1))
                if li < 2:
                    p.wait(sync, "sy", tok_weights)
                p.inc(sync.dma_start(out=buf[:], in_=src), f"wst{li % 2}", 16)
            ltok = p.tok(f"wst{li % 2}")

            @block.vector
            def _(vector, li=li, buf=buf, ltok=ltok, xroto=xroto):
                p.wait(vector, "v", ltok)
                p.wait(vector, "v", tok_xtr_t.get(li - 1))
                p.inc(vector.tensor_copy(out=AB[:, xroto:xroto + 1024], in_=buf[:]), "v")
            tok_xcast[li] = p.tok("v")

            for c in range(8):
                @block.tensor
                def _(tensor, li=li, c=c, xroto=xroto):
                    p.wait(tensor, "t", tok_xcast[li])
                    p.wait(tensor, "t", tok_xfm_ev.get(li * 8 + c - 2))
                    p.inc(tensor.transpose(pbT[c % 2][:, 128:256],
                                           AB[:, xroto + c * 128:xroto + (c + 1) * 128],
                                           ident[:]), "t")
                tok_xtr_t[(li, c)] = p.tok("t")

                @block.vector
                def _(vector, li=li, c=c):
                    p.wait(vector, "v", tok_xtr_t[(li, c)])
                    if c == 0:
                        p.wait(vector, "v", tok_projmm.get(li - 1))
                    p.inc(vector.tensor_copy(out=AB[:, QXROT + c * 128:QXROT + (c + 1) * 128],
                                             in_=pbT[c % 2][:, 128:256]), "v")
                tok_xfm_ev[li * 8 + c] = p.tok("v")
            tok_xtr_t[li] = p.tok("t")
            fm_tok = p.tok("v")

            wt_off = WKVT if kind == "kv" else WQT

            @block.tensor
            def _(tensor, li=li, fm_tok=fm_tok, wt_off=wt_off):
                p.wait(tensor, "t", fm_tok)
                p.wait(tensor, "t", tok_weights)
                p.wait(tensor, "t", tok_rope_psum_free.get(li - 1))
                for c in range(8):
                    lhsT = AB[:, QXROT + c * 128:QXROT + (c + 1) * 128]
                    p.inc(tensor.matmul(pb[0][:, 0:512], lhsT,
                                        AB[:, wt_off + c * 1024:wt_off + c * 1024 + 512],
                                        start=(c == 0), stop=(c == 7)), "t")
                    p.inc(tensor.matmul(pb[1][:, 0:512], lhsT,
                                        AB[:, wt_off + c * 1024 + 512:wt_off + (c + 1) * 1024],
                                        start=(c == 0), stop=(c == 7)), "t")
            tok_projmm[li] = p.tok("t")

            for half in range(2):
                bank = pb[half]
                ps4 = bank[:].rearrange("p (h s d) -> p h s d", s=2, d=32)
                ps3 = bank[:].rearrange("p (hs d) -> p hs d", d=32)
                cslice = ctb[li % 2][:]
                sslice = stb[li % 2][:]

                @block.vector
                def _(vector, key=key, half=half, li=li, ps3=ps3, ps4=ps4,
                      cslice=cslice, sslice=sslice, kind=kind, tt=tt):
                    p.wait(vector, "v", tok_projmm[li])
                    p.wait(vector, "v", tok_rot_cons.get(li * 2 + half))
                    p.wait(vector, "v", tok_tbl_sin[li])
                    if kind == "q":
                        p.wait(vector, "v", tok_qtr_t.get(("q", tt, half - 1)))
                        if half == 0:
                            p.wait(vector, "v", tok_qtr_t.get(("q", tt - 1, 1)))
                    rot4 = sc_a[:, 0:512].rearrange("p (h s d) -> p h s d", s=2, d=32)
                    rot3 = sc_a[:, 0:512].rearrange("p (hs d) -> p hs d", d=32)
                    t13 = sc_a[:, 512:1024].rearrange("p (hs d) -> p hs d", d=32)
                    nslice = ntb[li % 2][:]
                    p.inc(vector.tensor_tensor(out=rot4[:, :, 0, :], in0=ps4[:, :, 1, :],
                                               in1=nslice[:, None, :].broadcast_to((128, 8, 32)),
                                               op=ALU.mult), "v")
                    p.inc(vector.tensor_tensor(out=rot4[:, :, 1, :], in0=ps4[:, :, 0, :],
                                               in1=sslice[:, None, :].broadcast_to((128, 8, 32)),
                                               op=ALU.mult), "v")
                    p.inc(vector.tensor_tensor(out=t13, in0=ps3,
                                               in1=cslice[:, None, :].broadcast_to((128, 16, 32)),
                                               op=ALU.mult), "v")
                    p.wait(vector, "v", p.tok("v"))
                    if kind == "q":
                        p.inc(vector.tensor_tensor(
                            out=AB[:, PSB:PSB + 512].rearrange("p (hs d) -> p hs d", d=32),
                            in0=t13, in1=rot3, op=ALU.add), "v")
                    else:
                        p.inc(vector.tensor_tensor(
                            out=zb[:, half * 512:half * 512 + 512].rearrange("p (hs d) -> p hs d", d=32),
                            in0=t13, in1=rot3, op=ALU.add), "v")
                tok_roped = p.tok("v")
                tok_rot_cons[li * 2 + half + 1] = tok_roped
                if half == 1:
                    tok_tbl_free[li] = tok_roped

                if kind == "kv":
                    zslice = zb[:, half * 512:half * 512 + 512]
                    z3 = zslice.rearrange("p (h d) -> p h d", d=64)

                    @block.vector
                    def _(vector, z3=z3, zslice=zslice, tok_roped=tok_roped):
                        p.wait(vector, "v", tok_roped)
                        p.inc(vector.tensor_tensor(out=sc_a[:, 0:512], in0=zslice,
                                                   in1=zslice, op=ALU.mult), "v")
                        p.wait(vector, "v", p.tok("v"))
                        p.inc(vector.reduce_sum(out=mu8[:, 0:8], in_=z3, axis=AX.X), "v")
                        p.inc(vector.reduce_sum(out=s28[:, 0:8],
                                                in_=sc_a[:, 0:512].rearrange("p (h d) -> p h d", d=64),
                                                axis=AX.X), "v")
                        p.wait(vector, "v", p.tok("v"))
                        p.inc(vector.tensor_scalar(out=mu8[:, 0:8], in0=mu8[:, 0:8],
                                                   scalar1=1.0 / 64, scalar2=None, op0=ALU.mult), "v")
                        p.wait(vector, "v", p.tok("v"))
                        p.inc(vector.scalar_tensor_tensor(out=var8[:, 0:8], in0=mu8[:, 0:8],
                                                          scalar=1.0, in1=mu8[:, 0:8],
                                                          op0=ALU.mult, op1=ALU.mult), "v")
                        p.inc(vector.tensor_scalar(out=s28[:, 0:8], in0=s28[:, 0:8],
                                                   scalar1=1.0 / 64, scalar2=None, op0=ALU.mult), "v")
                        p.wait(vector, "v", p.tok("v"))
                        p.inc(vector.tensor_tensor(out=var8[:, 0:8], in0=s28[:, 0:8],
                                                   in1=var8[:, 0:8], op=ALU.subtract), "v")
                    tok_var = p.tok("v")

                    @block.scalar
                    def _(scalar, tok_var=tok_var):
                        p.wait(scalar, "s", tok_var)
                        p.inc(scalar.activation(out=sd8[:, 0:8], in_=var8[:, 0:8],
                                                func=AF.Sqrt, bias=eps_c[:, 0:1]), "s")
                    tok_sd = p.tok("s")

                    @block.vector
                    def _(vector, tt=tt, half=half, z3=z3, zslice=zslice, tok_sd=tok_sd):
                        p.wait(vector, "v", tok_sd)
                        p.inc(vector.reciprocal(out=rstd8[:, 0:8], in_=sd8[:, 0:8]), "v")
                        p.wait(vector, "v", p.tok("v"))
                        p.inc(vector.tensor_tensor(out=z3, in0=z3,
                                                   in1=mu8[:, 0:8, None].broadcast_to((128, 8, 64)),
                                                   op=ALU.subtract), "v")
                        p.wait(vector, "v", p.tok("v"))
                        p.inc(vector.tensor_tensor(out=z3, in0=z3,
                                                   in1=rstd8[:, 0:8, None].broadcast_to((128, 8, 64)),
                                                   op=ALU.mult), "v")
                        p.wait(vector, "v", p.tok("v"))
                        p.inc(vector.tensor_tensor(out=z3, in0=z3,
                                                   in1=gbb[:, half * 512:half * 512 + 512]
                                                   .rearrange("p (h d) -> p h d", d=64),
                                                   op=ALU.mult), "v")
                        p.wait(vector, "v", p.tok("v"))
                        p.inc(vector.tensor_tensor(
                            out=AB[:, R1 + tt * LKVS + half * 520: R1 + tt * LKVS + half * 520 + 520]
                            .rearrange("p (h d) -> p h d", d=65)[:, :, 0:64],
                            in0=zslice.rearrange("p (h d) -> p h d", d=64),
                            in1=bbb[:, half * 512:half * 512 + 512].rearrange("p (h d) -> p h d", d=64),
                            op=ALU.add), "v")
                    tok_lkv[(tt, half)] = p.tok("v")
                    if half == 1:
                        tok_rope_psum_free[li] = p.tok("v")
                else:
                    if half == 1:
                        tok_rope_psum_free[li] = tok_roped

                    for c in range(4):
                        cc = half * 4 + c

                        @block.tensor
                        def _(tensor, tt=tt, half=half, c=c, cc=cc, tok_roped=tok_roped):
                            p.wait(tensor, "t", tok_roped)
                            p.wait(tensor, "t", tok_qfm_lin.get(tt * 8 + cc - 2))
                            p.inc(tensor.transpose(pbT[cc % 2][:, 256:384],
                                                   AB[:, PSB + c * 128:PSB + (c + 1) * 128],
                                                   ident[:]), "t")
                        tok_qtr_t[("q", tt, half, c)] = p.tok("t")

                        @block.vector
                        def _(vector, tt=tt, cc=cc, half=half, c=c):
                            p.wait(vector, "v", tok_qtr_t[("q", tt, half, c)])
                            p.inc(vector.tensor_copy(
                                out=AB[:, R2 + cc * 2048 + tt * 128: R2 + cc * 2048 + (tt + 1) * 128],
                                in_=pbT[cc % 2][:, 256:384]), "v")
                        tok_qfm[("q", tt, cc)] = p.tok("v")
                        tok_qfm_lin[tt * 8 + cc] = p.tok("v")
                    tok_qtr_t[("q", tt, half)] = p.tok("t")
        tok_p1_v = None

        if "lkv" in dbg:
            @block.sync
            def _(sync):
                p.wait(sync, "sy", tok_p1_v)
                p.inc(sync.dma_start(out=dbg_outs["lkv"], in_=AB[:, R1:R1 + NT * 1024]), "dS", 16)
        if "qfm" in dbg:
            @block.sync
            def _(sync):
                p.wait(sync, "sy", tok_p1_v)
                p.inc(sync.dma_start(out=dbg_outs["qfm"], in_=AB[:, R2:R2 + NQ * 1024]), "dS", 16)

        tok_lkvf_t, tok_lkvf_ev = {}, {}
        tok_scmm, tok_exp, tok_avmm_g, tok_avn = {}, {}, {}, {}
        tok_rcb = {}
        g_all = [0]
        av_lin = [0]
        tok_pair_scores_t = {}
        SLOTP = [XROT, XROT + 512, XROT + 1024]

        def pair_kts(j):
            return 16 + 4 * j if j < 5 else CTXT


        def emit_pair_tr(pr, kt_lo, kt_hi):
            for kt in range(kt_lo, kt_hi):
                @block.tensor
                def _(tensor, pr=pr, kt=kt):
                    p.wait(tensor, "t", tok_lkv.get((kt, pr // 4)))
                    p.wait(tensor, "t", tok_lkvf_ev.get((pr, kt - 2)))
                    for hh in range(2):
                        co = R1 + kt * LKVS + (2 * pr + hh) * 65
                        p.inc(tensor.transpose(
                            pbT[kt % 2][64 * hh:64 * hh + 64, 384:512],
                            AB[:, co:co + 64], ident[:]), "t")
                tok_lkvf_t[(pr, kt)] = p.tok("t")

                @block.vector
                def _(vector, pr=pr, kt=kt):
                    p.wait(vector, "v", tok_lkvf_t[(pr, kt)])
                    if kt == 0 and pr > 0:
                        p.wait(vector, "v", tok_pair_scores_t[pr - 1])
                    p.inc(vector.tensor_copy(out=AB[:, LKVF + kt * 128:LKVF + (kt + 1) * 128],
                                             in_=pbT[kt % 2][:, 384:512]), "v")
                tok_lkvf_ev[(pr, kt)] = p.tok("v")

        def emit_pair_strands(pr, j_list, tr_after=None, defer_out=None):
            exp_defer = []
            norm_defer = [None]
            for j in j_list:
                if tr_after and j in tr_after:
                    emit_pair_tr(pr, *tr_after[j])
                nkt = pair_kts(j)
                qcol = R2 + pr * 2048 + j * 256
                for hh in range(2):
                    hd = 2 * pr + hh
                    base = 64 * hh
                    al = av_lin[0]; av_lin[0] += 1
                    av_bank = pb[2 + (al % 2)]
                    groups = []
                    for kt0 in range(0, nkt, 2):
                        kts = [kt0] + ([kt0 + 1] if kt0 + 1 < nkt else [])
                        g = g_all[0]; g_all[0] += 1
                        sc_bank = [pb[0], pb[1], pb[4]][g % 3]
                        poff = SLOTP[g % 3]
                        groups.append((g, kts, sc_bank, poff))

                    def emit_score(gi):
                        g, kts, sc_bank, poff = groups[gi]

                        @block.tensor
                        def _(tensor, pr=pr, base=base, kts=kts, g=g,
                              sc_bank=sc_bank, qcol=qcol):
                            p.wait(tensor, "t", tok_lkvf_ev[(pr, kts[-1])])
                            p.wait(tensor, "t", tok_qfm[("q", 2 * j + 1, pr)])
                            p.wait(tensor, "t", tok_exp.get(g - 3))
                            for i, kt in enumerate(kts):
                                p.inc(tensor.matmul(
                                    sc_bank[:, i * 256:(i + 1) * 256],
                                    AB[base:base + 64, LKVF + kt * 128:LKVF + (kt + 1) * 128],
                                    AB[base:base + 64, qcol:qcol + 256],
                                    start=True, stop=True), "t")
                        tok_scmm[g] = p.tok("t")
                        tok_exp[g] = p.pre("s")
                        exp_defer.append((g, len(kts), sc_bank, poff))

                    def emit_av(gi):
                        g, kts, sc_bank, poff = groups[gi]
                        mtok = tok_exp[g]
                        if j < 5:
                            orig0 = 12 + 4 * j
                            for i, kt in enumerate(kts):
                                for sl in range(2):
                                    mm = None
                                    if sl == 0 and kt == orig0:
                                        mm = mask_a
                                    elif sl == 0 and kt == orig0 + 1:
                                        mm = mask_b
                                    elif sl == 0 and kt >= orig0 + 2:
                                        mm = "zero"
                                    elif sl == 1 and kt == orig0 + 2:
                                        mm = mask_a
                                    elif sl == 1 and kt == orig0 + 3:
                                        mm = mask_b
                                    if mm is None:
                                        continue

                                    @block.vector
                                    def _(vector, i=i, sl=sl, poff=poff, mm=mm, g=g):
                                        p.wait(vector, "v", tok_exp[g])
                                        off = poff + i * 256 + sl * 128
                                        if mm == "zero":
                                            p.inc(vector.memset(AB[:, off:off + 128], 0.0), "v")
                                        else:
                                            p.inc(vector.tensor_tensor(
                                                out=AB[:, off:off + 128],
                                                in0=AB[:, off:off + 128],
                                                in1=mm[:], op=ALU.mult), "v")
                                    mtok = p.tok("v")

                        @block.tensor
                        def _(tensor, hd=hd, kts=kts, g=g, al=al, nkt=nkt,
                              av_bank=av_bank, mtok=mtok, poff=poff):
                            p.wait(tensor, "t", mtok)
                            p.wait(tensor, "t", tok_avn.get(al - 2))
                            for i, kt in enumerate(kts):
                                p.inc(tensor.matmul(
                                    av_bank[0:65, 0:256],
                                    AB[:, R1 + kt * LKVS + hd * 65: R1 + kt * LKVS + (hd + 1) * 65],
                                    AB[:, poff + i * 256:poff + (i + 1) * 256],
                                    start=(kt == 0), stop=(kt == nkt - 1)), "t")
                        tok_avmm_g[g] = p.tok("t")

                    ng = len(groups)
                    for gi in range(ng + 2):
                        if gi < ng:
                            emit_score(gi)
                        if gi == 2 and norm_defer[0] is not None:
                            norm_defer[0]()
                            norm_defer[0] = None
                        if gi >= 2:
                            emit_av(gi - 2)
                    tok_av_done = p.tok("t")

                    def emit_norm(hd=hd, hh=hh, pr=pr, j=j, al=al, av_bank=av_bank,
                                  tok_av_done=tok_av_done):
                        @block.vector
                        def _(vector, hd=hd, hh=hh, pr=pr, j=j, al=al, av_bank=av_bank,
                              tok_av_done=tok_av_done):
                            p.wait(vector, "v", tok_av_done)
                            p.wait(vector, "v", p.tok("v"))
                            p.inc(vector.reciprocal(out=yb[0][0:1, 192:448], in_=av_bank[64:65, 0:256]), "v")
                        tok_recp = p.tok("v")

                        @block.tensor
                        def _(tensor, al=al, tok_recp=tok_recp):
                            p.wait(tensor, "t", tok_recp)
                            p.wait(tensor, "t", tok_rcb.get(al - 1))
                            p.inc(tensor.matmul(pb[5][0:64, 256:512], ones_f[0:1, 0:64], yb[0][0:1, 192:448],
                                                start=True, stop=True), "t")
                        tok_rcmm = p.tok("t")

                        @block.vector
                        def _(vector, hd=hd, hh=hh, pr=pr, j=j, al=al, av_bank=av_bank,
                              tok_rcmm=tok_rcmm):
                            p.wait(vector, "v", tok_rcmm)
                            p.wait(vector, "v", p.tok("v"))
                            p.inc(vector.tensor_copy(out=sc_a[0:64, 0:256], in_=pb[5][0:64, 256:512]), "v")
                            p.wait(vector, "v", p.tok("v"))
                            dst = R3 + (pr) * 2048 + j * 256
                            p.inc(vector.tensor_tensor(
                                out=AB[64 * hh:64 * hh + 64, dst:dst + 256],
                                in0=av_bank[0:64, 0:256], in1=sc_a[0:64, 0:256],
                                op=ALU.mult), "v")
                        tok_avn[al] = p.tok("v")
                        tok_rcb[al] = p.tok("v")
                    norm_defer[0] = emit_norm
            if norm_defer[0] is not None:
                norm_defer[0]()
                norm_defer[0] = None

            if defer_out is not None:
                defer_out.extend(exp_defer)
                return

            @block.scalar
            def _(scalar, exp_defer=list(exp_defer)):
                for g, nk, bank, poff in exp_defer:
                    p.wait(scalar, "s", tok_scmm[g])
                    p.wait(scalar, "s", tok_avmm_g.get(g - 3))
                    p.emit_pre(scalar.activation(out=AB[:, poff:poff + nk * 256],
                                                 in_=bank[:, 0:nk * 256],
                                                 func=AF.Exp), "s")

        def emit_pair_finish(pr):
            tok_pair_scores_t[pr] = p.tok("t")

        for s in range(NQ):
            emit_proj_tile("q", s)
        for t in range(12):
            emit_proj_tile("kv", t)
        emit_pair_tr(0, 0, 12)
        emit_pair_strands(0, [5, 6, 7])
        for t0, jj in ((12, 0), (16, 1), (20, 2), (24, 3), (28, 4)):
            for t in range(t0, t0 + 4):
                emit_proj_tile("kv", t)
            emit_pair_tr(0, t0, t0 + 4)
            emit_pair_strands(0, [jj])
        emit_pair_finish(0)
        tok_p1_v = p.tok("v")
        p2_exps = []
        for pr in range(1, 8):
            emit_pair_tr(pr, 0, 12)
            emit_pair_strands(pr, [5, 6, 7, 0, 1, 2, 3, 4],
                              tr_after={7: (12, 16), 0: (16, 22), 1: (22, 26),
                                        2: (26, 30), 3: (30, 32)},
                              defer_out=p2_exps)
            emit_pair_finish(pr)

        @block.scalar
        def _(scalar, exp_defer=list(p2_exps)):
            for g, nk, bank, poff in exp_defer:
                p.wait(scalar, "s", tok_scmm[g])
                p.wait(scalar, "s", tok_avmm_g.get(g - 3))
                p.emit_pre(scalar.activation(out=AB[:, poff:poff + nk * 256],
                                             in_=bank[:, 0:nk * 256],
                                             func=AF.Exp), "s")
        tok_p2_v = p.tok("v")

        @block.sync
        def _(sync):
            p.wait(sync, "sy", tok_p2_v)
            p.inc(sync.dma_start(out=yb[0][0:1, :], in_=bo_in), "yb0", 16)
        boldk = p.tok("yb0")

        @block.vector
        def _(vector):
            p.wait(vector, "v", boldk)
            p.inc(vector.tensor_copy(out=AB[0:1, LKVF:LKVF + 1024], in_=yb[0][0:1, :]), "v")
        tok_bo = p.tok("v")

        if "avfm" in dbg:
            @block.sync
            def _(sync):
                p.wait(sync, "sy", tok_p2_v)
                p.inc(sync.dma_start(out=dbg_outs["avfm"], in_=AB[:, R3:R3 + 16384]), "dS", 16)

        tok_womm, tok_z3, tok_sq3, tok_yv, tok_ydma = {}, {}, {}, {}, {}
        for s_ in range(NQ):
            ba, bb_ = (pb[4], pb[5]) if s_ % 2 == 0 else (pb[2], pb[3])

            @block.sync
            def _(sync, s_=s_):
                p.wait(sync, "sy", tok_z3.get(s_ - 1))
                if s_ == 0:
                    p.wait(sync, "sy", tok_p1_v)
                p.inc(sync.dma_start(out=xob[s_ % 2][:], in_=xr_in[s_ * 128:(s_ + 1) * 128, :]),
                      f"xob{s_ % 2}", 16)
            xtok = p.tok(f"xob{s_ % 2}")

            @block.tensor
            def _(tensor, s_=s_, ba=ba, bb_=bb_):
                p.wait(tensor, "t", tok_consts_v)
                p.wait(tensor, "t", tok_p2_v)
                p.wait(tensor, "t", tok_bo)
                for c in range(8):
                    p.wait(tensor, "t", tok_avn[c * 16 + (s_ // 2) * 2 + 0])
                    p.wait(tensor, "t", tok_avn[c * 16 + (s_ // 2) * 2 + 1])
                p.wait(tensor, "t", tok_z3.get(s_ - 2))
                for c in range(8):
                    lhsT = AB[:, R3 + c * 2048 + s_ * 128: R3 + c * 2048 + (s_ + 1) * 128]
                    p.inc(tensor.matmul(ba[:, 0:512], lhsT,
                                        AB[:, WOT + c * 1024:WOT + c * 1024 + 512],
                                        start=(c == 0), stop=False), "t")
                    p.inc(tensor.matmul(bb_[:, 0:512], lhsT,
                                        AB[:, WOT + c * 1024 + 512:WOT + (c + 1) * 1024],
                                        start=(c == 0), stop=False), "t")
                p.inc(tensor.matmul(ba[:, 0:512], ones_b[:], AB[0:1, LKVF:LKVF + 512],
                                    start=False, stop=True), "t")
                p.inc(tensor.matmul(bb_[:, 0:512], ones_b[:], AB[0:1, LKVF + 512:LKVF + 1024],
                                    start=False, stop=True), "t")
            tok_womm[s_] = p.tok("t")

            @block.vector
            def _(vector, s_=s_, xtok=xtok, ba=ba, bb_=bb_):
                p.wait(vector, "v", tok_womm[s_])
                p.wait(vector, "v", xtok)
                p.wait(vector, "v", tok_sq3.get(s_ - 1))
                xo = xob[s_ % 2]
                p.inc(vector.tensor_tensor(out=zb[:, 0:512], in0=ba[:, 0:512],
                                           in1=xo[:, 0:512], op=ALU.add), "v")
                p.inc(vector.tensor_tensor(out=zb[:, 512:1024], in0=bb_[:, 0:512],
                                           in1=xo[:, 512:1024], op=ALU.add), "v")
            tok_z3[s_] = p.tok("v")

            @block.vector
            def _(vector, s_=s_):
                p.wait(vector, "v", tok_z3[s_])
                p.inc(vector.scalar_tensor_tensor(out=sc_a[:], in0=zb[:], scalar=1.0,
                                                  in1=zb[:], op0=ALU.mult, op1=ALU.mult,
                                                  accum_out=s2c[:]), "v")
            tok_sq3[s_] = p.tok("v")

            @block.vector
            def _(vector, s_=s_):
                p.wait(vector, "v", tok_z3[s_])
                p.inc(vector.reduce_sum(out=sum1[:], in_=zb[:], axis=AX.X), "v")
                p.wait(vector, "v", tok_sq3[s_])
                p.wait(vector, "v", p.tok("v"))
                p.inc(vector.tensor_scalar(out=mu1[:], in0=sum1[:], scalar1=1.0 / D,
                                           scalar2=None, op0=ALU.mult), "v")
                p.wait(vector, "v", p.tok("v"))
                p.inc(vector.scalar_tensor_tensor(out=tmp1[:], in0=mu1[:], scalar=1.0,
                                                  in1=mu1[:], op0=ALU.mult, op1=ALU.mult), "v")
                p.inc(vector.tensor_scalar(out=var1[:], in0=s2c[:], scalar1=1.0 / D,
                                           scalar2=None, op0=ALU.mult), "v")
                p.wait(vector, "v", p.tok("v"))
                p.inc(vector.tensor_tensor(out=var1[:], in0=var1[:], in1=tmp1[:],
                                           op=ALU.subtract), "v")
            tok_var3 = p.tok("v")

            @block.scalar
            def _(scalar, tok_var3=tok_var3):
                p.wait(scalar, "s", tok_var3)
                p.inc(scalar.activation(out=sd1[:], in_=var1[:], func=AF.Sqrt,
                                        bias=eps_c[:, 0:1]), "s")
            tok_sd3 = p.tok("s")

            @block.vector
            def _(vector, s_=s_, tok_sd3=tok_sd3):
                p.wait(vector, "v", tok_sd3)
                p.wait(vector, "v", tok_ydma.get(s_ - 2))
                p.inc(vector.reciprocal(out=rstd1[:], in_=sd1[:]), "v")
                p.wait(vector, "v", p.tok("v"))
                yv = yb[s_ % 2]
                p.inc(vector.tensor_scalar(out=yv[:], in0=zb[:], scalar1=mu1[:, 0:1],
                                           scalar2=rstd1[:, 0:1], op0=ALU.subtract,
                                           op1=ALU.mult), "v")
                p.wait(vector, "v", p.tok("v"))
                p.wait(vector, "v", p.tok("v"))
                p.inc(vector.reduce_max(out=rmax[:], in_=yv[:], axis=AX.X,
                                        apply_absolute_value=True), "v")
                p.wait(vector, "v", p.tok("v"))
                if s_ == 0:
                    p.inc(vector.tensor_copy(out=ymax[:], in_=rmax[:]), "v")
                else:
                    p.inc(vector.tensor_tensor(out=ymax[:], in0=ymax[:], in1=rmax[:],
                                               op=ALU.max), "v")
            tok_yv[s_] = p.tok("v")

            @block.sync
            def _(sync, s_=s_):
                p.wait(sync, "sy", tok_yv[s_])
                p.inc(sync.dma_start(out=y_d[s_ * 128:(s_ + 1) * 128, :], in_=yb[s_ % 2][:]),
                      f"yb{s_ % 2}", 16)
            tok_ydma[s_] = p.tok(f"yb{s_ % 2}")
        tok_p3_v = p.tok("v")

        tok_wld, tok_wmaxv = {}, {}
        for i in range(64):
            @block.gpsimd
            def _(gpsimd, i=i):
                p.wait(gpsimd, "g", tok_wmaxv.get(i - 1))
                if i < 2:
                    p.wait(gpsimd, "g", tok_p1_v)
                if i < 32:
                    wsrc = w1_in[i * 128:(i + 1) * 128, :]
                else:
                    r, hc = divmod(i - 32, 4)
                    wsrc = w2_in[r * 128:(r + 1) * 128, hc * 1024:(hc + 1) * 1024]
                p.inc(gpsimd.dma_start(out=wst[i % 2][:], in_=wsrc), f"wst{i % 2}P", 16)
                tok_wld[i] = p.tok(f"wst{i % 2}P")

            @block.vector
            def _(vector, i=i):
                if i == 0:
                    p.wait(vector, "v", tok_p1_v)
                p.wait(vector, "v", tok_wld[i])
                p.wait(vector, "v", p.tok("v"))
                p.inc(vector.reduce_max(out=rmax[:], in_=wst[i % 2][:], axis=AX.X,
                                        apply_absolute_value=True), "v")
                p.wait(vector, "v", p.tok("v"))
                dst = wmax1 if i < 32 else wmax2
                if i in (0, 32):
                    p.inc(vector.tensor_copy(out=dst[:], in_=rmax[:]), "v")
                else:
                    p.inc(vector.tensor_tensor(out=dst[:], in0=dst[:], in1=rmax[:],
                                               op=ALU.max), "v")
                tok_wmaxv[i] = p.tok("v")
        tok_wmax = p.tok("v")



        @block.tensor
        def _(tensor):
            p.wait(tensor, "t", tok_p3_v)
            p.wait(tensor, "t", tok_wmax)
            p.wait(tensor, "t", tok_idf)
            p.inc(tensor.matmul(pb[0][0:1, 0:128], ymax[:], identf[:],
                                start=True, stop=True), "t")
            p.inc(tensor.matmul(pb[1][0:1, 0:128], wmax1[:], identf[:],
                                start=True, stop=True), "t")
            p.inc(tensor.matmul(pb[0][0:1, 128:256], wmax2[:], identf[:],
                                start=True, stop=True), "t")
        tok_pr_mm = p.tok("t")

        @block.vector
        def _(vector):
            p.wait(vector, "v", tok_pr_mm)
            p.inc(vector.reduce_max(out=gm1[:], in_=pb[0][0:1, 0:128], axis=AX.X), "v")
            p.inc(vector.reduce_max(out=wm1[:], in_=pb[1][0:1, 0:128], axis=AX.X), "v")
            p.inc(vector.reduce_max(out=wm2[:], in_=pb[0][0:1, 128:256], axis=AX.X), "v")
            p.wait(vector, "v", p.tok("v"))
            p.inc(vector.tensor_scalar(out=row16[:], in0=ones_f[0:1, 0:16],
                                       scalar1=gm1[0:1, 0:1], scalar2=None, op0=ALU.mult), "v")
        tok_gm1row = p.tok("v")

        @block.vector
        def _(vector):
            p.wait(vector, "v", tok_gm1row)
            p.inc(vector.tensor_scalar(out=wm1[:], in0=wm1[:], scalar1=1e-6, scalar2=None,
                                       op0=ALU.max), "v")
            p.inc(vector.tensor_scalar(out=wm2[:], in0=wm2[:], scalar1=1e-6, scalar2=None,
                                       op0=ALU.max), "v")
            p.wait(vector, "v", p.tok("v"))
            p.inc(vector.reciprocal(out=rsw1[:], in_=wm1[:]), "v")
            p.inc(vector.reciprocal(out=rsw2[:], in_=wm2[:]), "v")
            p.wait(vector, "v", p.tok("v"))
            p.inc(vector.tensor_scalar(out=rsw1[:], in0=rsw1[:], scalar1=127.0, scalar2=None,
                                       op0=ALU.mult), "v")
            p.inc(vector.tensor_scalar(out=rsw2[:], in0=rsw2[:], scalar1=127.0, scalar2=None,
                                       op0=ALU.mult), "v")
        tok_rsw = p.tok("v")

        @block.tensor
        def _(tensor):
            p.wait(tensor, "t", tok_rsw)
            p.wait(tensor, "t", tok_avn[127])
            p.inc(tensor.matmul(pb[2][:, 1:2], ones_f[:], rsw1[:], start=True, stop=True), "t")
            p.inc(tensor.matmul(pb[2][:, 2:3], ones_f[:], rsw2[:], start=True, stop=True), "t")
        tok_bcw_mm = p.tok("t")

        @block.vector
        def _(vector):
            p.wait(vector, "v", tok_bcw_mm)
            p.inc(vector.tensor_copy(out=rsw1_b[:], in_=pb[2][:, 1:2]), "v")
            p.inc(vector.tensor_copy(out=rsw2_b[:], in_=pb[2][:, 2:3]), "v")
        tok_bcw = p.tok("v")

        @block.gpsimd
        def _(gpsimd):
            p.wait(gpsimd, "g", tok_gm1row)
            p.inc(gpsimd.dma_start(out=cc1i.ap(), in_=row16[:]), "dP", 16)
            gpsimd.wait_ge(p.sems["dP"], p.cnt["dP"])
            p.lastw[("gp", "dP")] = p.cnt["dP"]
            p.inc(gpsimd.collective_compute(
                "AllReduce", ALU.max, replica_groups=[list(range(8))],
                ins=[cc1i.ap().opt()], outs=[cc1o.ap().opt()]), "cc")

        @block.scalar
        def _(scalar):
            p.wait(scalar, "s", ("cc", p.cnt["cc"]))
            p.inc(scalar.dma_start(out=row16[:], in_=cc1o.ap()), "dA", 16)
        tok_cc1 = p.tok("dA")

        @block.vector
        def _(vector):
            p.wait(vector, "v", tok_cc1)
            p.inc(vector.tensor_scalar(out=gm1[:], in0=row16[0:1, 0:1], scalar1=1e-6,
                                       scalar2=None, op0=ALU.max), "v")
            p.wait(vector, "v", p.tok("v"))
            p.inc(vector.tensor_scalar(out=sx_s[:], in0=gm1[:], scalar1=1.0 / 127, scalar2=None,
                                       op0=ALU.mult), "v")
            p.inc(vector.reciprocal(out=rsx[:], in_=gm1[:]), "v")
            p.inc(vector.tensor_scalar(out=s1_s[:], in0=wm1[:], scalar1=1.0 / 127,
                                       scalar2=None, op0=ALU.mult), "v")
            p.wait(vector, "v", p.tok("v"))
            p.inc(vector.tensor_scalar(out=rsx[:], in0=rsx[:], scalar1=127.0, scalar2=None,
                                       op0=ALU.mult), "v")
            p.inc(vector.tensor_tensor(out=s1_s[:], in0=s1_s[:], in1=sx_s[:], op=ALU.mult), "v")
        tok_scales1 = p.tok("v")

        @block.tensor
        def _(tensor):
            p.wait(tensor, "t", tok_scales1)
            p.inc(tensor.matmul(pb[2][:, 0:1], ones_f[:], rsx[:], start=True, stop=True), "t")
            p.inc(tensor.matmul(pb[2][:, 3:4], ones_f[:], s1_s[:], start=True, stop=True), "t")
        tok_bc1mm = p.tok("t")

        @block.vector
        def _(vector):
            p.wait(vector, "v", tok_bc1mm)
            p.inc(vector.tensor_copy(out=rsx_b[:], in_=pb[2][:, 0:1]), "v")
            p.inc(vector.tensor_copy(out=s1_b[:], in_=pb[2][:, 3:4]), "v")
            p.wait(vector, "v", p.tok("v"))
            p.inc(vector.tensor_tensor(out=b1s[:], in0=b1c[:],
                                       in1=s1_b[:, 0:1].broadcast_to((128, 32)), op=ALU.mult), "v")
        tok_bc1 = p.tok("v")

        if "scal" in dbg:
            @block.vector
            def _(vector):
                p.wait(vector, "v", tok_bc1)
                p.inc(vector.tensor_copy(out=row16[0:1, 0:1], in_=sx_s[:]), "v")
                p.inc(vector.tensor_copy(out=row16[0:1, 1:2], in_=wm1[:]), "v")
                p.inc(vector.tensor_copy(out=row16[0:1, 2:3], in_=wm2[:]), "v")
                p.inc(vector.tensor_copy(out=row16[0:1, 3:4], in_=s1_s[:]), "v")
            tks = p.tok("v")

            @block.sync
            def _(sync):
                p.wait(sync, "sy", tks)
                p.inc(sync.dma_start(out=dbg_outs["scal"], in_=row16[:]), "dS", 16)

        def quant_1024(vector, src, rs_col, dst_off):
            for ch in range(4):
                sl = slice(ch * 256, (ch + 1) * 256)
                p.inc(vector.tensor_scalar(out=sc_a[:, sl], in0=src[:, sl],
                                           scalar1=rs_col, scalar2=256.0,
                                           op0=ALU.mult, op1=ALU.add), "v")
                p.wait(vector, "v", p.tok("v"))
                p.inc(vector.tensor_scalar(out=sc_i[:], in0=sc_a[:, sl], scalar1=0.5,
                                           scalar2=None, op0=ALU.add), "v")
                p.wait(vector, "v", p.tok("v"))
                p.inc(vector.tensor_tensor(out=sc_i2[:], in0=sc_i[:], in1=sc_a[:, sl],
                                           op=ALU.subtract), "v")
                p.wait(vector, "v", p.tok("v"))
                p.inc(vector.scalar_tensor_tensor(out=AB[:, dst_off + ch * 256: dst_off + (ch + 1) * 256],
                                                  in0=sc_i[:], scalar=256.0, in1=sc_i2[:],
                                                  op0=ALU.subtract, op1=ALU.subtract), "v")
                p.wait(vector, "v", p.tok("v"))

        tok_w1ld, tok_w1q, tok_w1tr_t, tok_w1T, tok_w1T_lin = {}, {}, {}, {}, {}

        for r in range(32):
            srcbuf = wst[r % 2] if r % 2 == 0 else xob[0]

            @block.gpsimd
            def _(gpsimd, r=r):
                p.wait(gpsimd, "g", tok_w1q.get(r - 1))
                if r == 0:
                    p.wait(gpsimd, "g", ("cc", 1))
                if r < 2:
                    p.wait(gpsimd, "g", tok_wmax)
                    p.wait(gpsimd, "g", tok_p3_v)
                sem = "wst0P" if r % 2 == 0 else "xob0P"
                p.inc(gpsimd.dma_start(out=(wst[0] if r % 2 == 0 else xob[0])[:],
                                       in_=w1_in[r * 128:(r + 1) * 128, :]), sem, 16)
                tok_w1ld[r] = p.tok(sem)

            @block.vector
            def _(vector, r=r, srcbuf=srcbuf):
                p.wait(vector, "v", tok_w1ld[r])
                p.wait(vector, "v", tok_bcw)
                p.wait(vector, "v", tok_p2_v)
                p.wait(vector, "v", tok_w1tr_t.get(r - 1))
                quant_1024(vector, srcbuf, rsw1_b[:, 0:1], QSTG)
            tok_w1q[r] = p.tok("v")

            for c in range(8):
                @block.tensor
                def _(tensor, r=r, c=c):
                    p.wait(tensor, "t", tok_w1q[r])
                    if r == 0 and c == 0:
                        p.wait(tensor, "t", tok_p2_v)
                    p.wait(tensor, "t", tok_w1T_lin.get(r * 8 + c - 2))
                    p.inc(tensor.transpose(pbT[c % 2][:, 768:896],
                                           AB[:, QSTG + c * 128:QSTG + (c + 1) * 128], ident[:]), "t")
                tok_w1tr_t[(r, c)] = p.tok("t")

                @block.vector
                def _(vector, r=r, c=c):
                    p.wait(vector, "v", tok_w1tr_t[(r, c)])
                    p.inc(vector.tensor_copy(
                        out=AB[:, R1 + c * 4096 + r * 128: R1 + c * 4096 + (r + 1) * 128],
                        in_=pbT[c % 2][:, 768:896]), "v")
                tok_w1T[(r, c)] = p.tok("v")
                tok_w1T_lin[r * 8 + c] = p.tok("v")
            tok_w1tr_t[r] = p.tok("t")
        tok_w1 = p.tok("v")

        tok_xqv, tok_xqtr_t, tok_xqfm, tok_xqfm_lin = {}, {}, {}, {}
        for s_ in range(NQ):
            @block.sync
            def _(sync, s_=s_):
                p.wait(sync, "sy", tok_ydma[NQ - 2])
                p.wait(sync, "sy", tok_ydma[NQ - 1])
                p.wait(sync, "sy", tok_xqv.get(s_ - 1))
                p.inc(sync.dma_start(out=yb[s_ % 2][:], in_=y_d[s_ * 128:(s_ + 1) * 128, :]),
                      f"yb{s_ % 2}", 16)
            ytok = p.tok(f"yb{s_ % 2}")

            @block.vector
            def _(vector, s_=s_, ytok=ytok):
                p.wait(vector, "v", ytok)
                p.wait(vector, "v", tok_bc1)
                p.wait(vector, "v", tok_w1tr_t[31])
                p.wait(vector, "v", tok_xqtr_t.get(s_ - 1))
                quant_1024(vector, yb[s_ % 2], rsx_b[:, 0:1], QSTG)
            tok_xqv[s_] = p.tok("v")

            for c in range(8):
                @block.tensor
                def _(tensor, s_=s_, c=c):
                    p.wait(tensor, "t", tok_xqv[s_])
                    p.wait(tensor, "t", tok_xqfm_lin.get(s_ * 8 + c - 2))
                    p.inc(tensor.transpose(pbT[c % 2][:, 640:768],
                                           AB[:, QSTG + c * 128:QSTG + (c + 1) * 128], ident[:]), "t")
                tok_xqtr_t[(s_, c)] = p.tok("t")

                @block.vector
                def _(vector, s_=s_, c=c):
                    p.wait(vector, "v", tok_xqtr_t[(s_, c)])
                    if s_ == 0 and c == 0:
                        p.wait(vector, "v", tok_p2_v)
                    p.inc(vector.tensor_copy(
                        out=AB[:, R3 + c * 2048 + s_ * 128: R3 + c * 2048 + (s_ + 1) * 128],
                        in_=pbT[c % 2][:, 640:768]), "v")
                tok_xqfm[(s_, c)] = p.tok("v")
                tok_xqfm_lin[s_ * 8 + c] = p.tok("v")
            tok_xqtr_t[s_] = p.tok("t")
        tok_xq_all = p.tok("v")

        if "xqd" in dbg:
            @block.sync
            def _(sync):
                p.wait(sync, "sy", tok_xq_all)
                p.inc(sync.dma_start(out=dbg_outs["xqd"], in_=AB[:, R3:R3 + 16384]), "dS", 16)

        tok_hmm, tok_hv, tok_hmaxv, tok_hdma = {}, {}, {}, {}
        for rb in range(8):
            gelu_defer = []
            for ht in range(32):
                n = rb * 32 + ht
                bank = pb[n % 2]

                @block.tensor
                def _(tensor, rb=rb, ht=ht, n=n, bank=bank):
                    for c in range(8):
                        p.wait(tensor, "t", tok_w1T[(ht, c)])
                        p.wait(tensor, "t", tok_xqfm[(2 * rb, c)])
                        p.wait(tensor, "t", tok_xqfm[(2 * rb + 1, c)])
                    p.wait(tensor, "t", tok_hv.get(n - 2))
                    for c in range(8):
                        p.inc(tensor.matmul(
                            bank[:, 0:256],
                            AB[:, R1 + c * 4096 + ht * 128: R1 + c * 4096 + (ht + 1) * 128],
                            AB[:, R3 + c * 2048 + rb * 256: R3 + c * 2048 + (rb + 1) * 256],
                            start=(c == 0), stop=(c == 7)), "t")
                tok_hmm[n] = p.tok("t")

                tok_hv[n] = p.pre("s")
                gelu_defer.append((n, ht, bank))

                @block.vector
                def _(vector, n=n):
                    p.wait(vector, "v", tok_hv[n])
                    p.wait(vector, "v", p.tok("v"))
                    p.inc(vector.reduce_max(out=rmax[:], in_=hb[n % 2][:], axis=AX.X,
                                            apply_absolute_value=True), "v")
                    p.wait(vector, "v", p.tok("v"))
                    if n == 0:
                        p.inc(vector.tensor_copy(out=hmax[:], in_=rmax[:]), "v")
                    else:
                        p.inc(vector.tensor_tensor(out=hmax[:], in0=hmax[:], in1=rmax[:],
                                                   op=ALU.max), "v")
                tok_hmaxv[n] = p.tok("v")

                @block.gpsimd
                def _(gpsimd, rb=rb, ht=ht, n=n):
                    p.wait(gpsimd, "g", tok_hv[n])
                    p.inc(gpsimd.dma_start(out=h_d[ht * 128:(ht + 1) * 128,
                                                  rb * 256:(rb + 1) * 256],
                                           in_=hb[n % 2][:]), f"hb{n % 2}", 16)
                tok_hdma[n] = p.tok(f"hb{n % 2}")

            @block.scalar
            def _(scalar, gelu_defer=list(gelu_defer)):
                for n, ht, bank in gelu_defer:
                    p.wait(scalar, "s", tok_hmm[n])
                    p.wait(scalar, "s", tok_hmaxv.get(n - 2))
                    p.wait(scalar, "s", tok_hdma.get(n - 2))
                    p.emit_pre(scalar.activation(out=hb[n % 2][:], in_=bank[:, 0:256],
                                                 func=AF.Gelu, bias=b1s[:, ht:ht + 1],
                                                 scale=s1_b[:, 0:1]), "s")
        tok_ffn1_s = p.tok("s")
        tok_ffn1_max = p.tok("v")
        tok_hdma_last = (p.tok("hb0"), p.tok("hb1"))

        tok_w2ld, tok_w2q, tok_w2tr_t, tok_w2T, tok_w2T_lin = {}, {}, {}, {}, {}

        for i in range(32):
            r, hc = divmod(i, 4)
            srcbuf = wst[i % 2] if i % 2 == 0 else xob[1]

            @block.gpsimd
            def _(gpsimd, i=i, r=r, hc=hc):
                p.wait(gpsimd, "g", tok_w2q.get(i - 1))
                if i < 2:
                    p.wait(gpsimd, "g", tok_w1)
                sem = "wst0P" if i % 2 == 0 else "xob1P"
                p.inc(gpsimd.dma_start(out=(wst[0] if i % 2 == 0 else xob[1])[:],
                                       in_=w2_in[r * 128:(r + 1) * 128, hc * 1024:(hc + 1) * 1024]), sem, 16)
                tok_w2ld[i] = p.tok(sem)

            @block.vector
            def _(vector, i=i, srcbuf=srcbuf):
                p.wait(vector, "v", tok_w2ld[i])
                p.wait(vector, "v", tok_w1)
                p.wait(vector, "v", tok_w2tr_t.get(i - 1))
                quant_1024(vector, srcbuf, rsw2_b[:, 0:1], QSTG)
            tok_w2q[i] = p.tok("v")

            for c in range(8):
                @block.tensor
                def _(tensor, i=i, c=c):
                    p.wait(tensor, "t", tok_w2q[i])
                    if i == 0 and c == 0:
                        p.wait(tensor, "t", tok_p3_v)
                    p.wait(tensor, "t", tok_w2T_lin.get(i * 8 + c - 2))
                    p.inc(tensor.transpose(pbT[c % 2][:, 896:1024],
                                           AB[:, QSTG + c * 128:QSTG + (c + 1) * 128], ident[:]), "t")
                tok_w2tr_t[(i, c)] = p.tok("t")

                @block.vector
                def _(vector, i=i, c=c, r=r, hc=hc):
                    ht = hc * 8 + c
                    dst = R2 if ht < 16 else R4
                    hto = ht if ht < 16 else ht - 16
                    p.wait(vector, "v", tok_w2tr_t[(i, c)])
                    p.inc(vector.tensor_copy(
                        out=AB[:, dst + hto * 1024 + r * 128: dst + hto * 1024 + (r + 1) * 128],
                        in_=pbT[c % 2][:, 896:1024]), "v")
                tok_w2T[(i, c)] = p.tok("v")
                tok_w2T_lin[i * 8 + c] = p.tok("v")
            tok_w2tr_t[i] = p.tok("t")
        tok_w2 = p.tok("v")

        @block.tensor
        def _(tensor):
            p.wait(tensor, "t", tok_ffn1_max)
            p.inc(tensor.matmul(pb[2][0:1, 0:128], hmax[:], identf[:],
                                start=True, stop=True), "t")
        tok_hm_mm = p.tok("t")

        @block.vector
        def _(vector):
            p.wait(vector, "v", tok_hm_mm)
            p.inc(vector.reduce_max(out=gm2[:], in_=pb[2][0:1, 0:128], axis=AX.X), "v")
            p.wait(vector, "v", p.tok("v"))
            p.inc(vector.tensor_scalar(out=row16[:], in0=ones_f[0:1, 0:16],
                                       scalar1=gm2[0:1, 0:1], scalar2=None, op0=ALU.mult), "v")
        tok_gm2row = p.tok("v")

        @block.gpsimd
        def _(gpsimd):
            p.wait(gpsimd, "g", tok_gm2row)
            p.inc(gpsimd.dma_start(out=cc2i.ap(), in_=row16[:]), "dP", 16)
            gpsimd.wait_ge(p.sems["dP"], p.cnt["dP"])
            p.lastw[("gp", "dP")] = p.cnt["dP"]
            p.inc(gpsimd.collective_compute(
                "AllReduce", ALU.max, replica_groups=[list(range(8))],
                ins=[cc2i.ap().opt()], outs=[cc2o.ap().opt()]), "cc")

        @block.scalar
        def _(scalar):
            p.wait(scalar, "s", ("cc", p.cnt["cc"]))
            p.inc(scalar.dma_start(out=row16[:], in_=cc2o.ap()), "dA", 16)
        tok_cc2 = p.tok("dA")

        @block.vector
        def _(vector):
            p.wait(vector, "v", tok_cc2)
            p.inc(vector.tensor_scalar(out=gm2[:], in0=row16[0:1, 0:1], scalar1=1e-6,
                                       scalar2=None, op0=ALU.max), "v")
            p.wait(vector, "v", p.tok("v"))
            p.inc(vector.tensor_scalar(out=sh_s[:], in0=gm2[:], scalar1=1.0 / 127, scalar2=None,
                                       op0=ALU.mult), "v")
            p.inc(vector.reciprocal(out=rsh[:], in_=gm2[:]), "v")
            p.inc(vector.tensor_scalar(out=s2_s[:], in0=wm2[:], scalar1=1.0 / 127,
                                       scalar2=None, op0=ALU.mult), "v")
            p.wait(vector, "v", p.tok("v"))
            p.inc(vector.tensor_scalar(out=rsh[:], in0=rsh[:], scalar1=127.0, scalar2=None,
                                       op0=ALU.mult), "v")
            p.inc(vector.tensor_tensor(out=s2_s[:], in0=s2_s[:], in1=sh_s[:], op=ALU.mult), "v")
        tok_scales2 = p.tok("v")

        @block.tensor
        def _(tensor):
            p.wait(tensor, "t", tok_scales2)
            p.inc(tensor.matmul(pb[2][:, 4:5], ones_f[:], rsh[:], start=True, stop=True), "t")
            p.inc(tensor.matmul(pb[2][:, 5:6], ones_f[:], s2_s[:], start=True, stop=True), "t")
        tok_bc2mm = p.tok("t")

        @block.vector
        def _(vector):
            p.wait(vector, "v", tok_bc2mm)
            p.inc(vector.tensor_copy(out=rsh_b[:], in_=pb[2][:, 4:5]), "v")
            p.inc(vector.tensor_copy(out=s2_b[:], in_=pb[2][:, 5:6]), "v")
        tok_bc2 = p.tok("v")

        @block.sync
        def _(sync):
            p.wait(sync, "sy", tok_w2)
            p.inc(sync.dma_start(out=xob[0][0:1, :], in_=b2_in), "xob0", 16)
        b2ld = p.tok("xob0")

        @block.vector
        def _(vector):
            p.wait(vector, "v", b2ld)
            p.wait(vector, "v", tok_w2)
            p.inc(vector.tensor_copy(out=AB[0:1, QSTG:QSTG + 1024], in_=xob[0][0:1, :]), "v")
        tok_b2 = p.tok("v")

        tok_hq, tok_omm, tok_z5, tok_sq5, tok_outv, tok_outdma = {}, {}, {}, {}, {}, {}
        def emit_quants(rb):
            hqo = HQ0 if rb % 2 == 0 else HQ1
            for ht in range(32):
                n = rb * 32 + ht

                hbufs = [hb[0][:], hb[1][:], sc_a[:, 0:256], sc_a[:, 256:512]]
                hsem = ["hb0", "hb1", "hb2", "hb3"][n % 4]

                @block.gpsimd
                def _(gpsimd, rb=rb, ht=ht, n=n, hsem=hsem):
                    p.wait(gpsimd, "g", tok_hq.get(n - 3))
                    p.wait(gpsimd, "g", tok_omm.get(rb - 2))
                    if n == 0:
                        p.wait(gpsimd, "g", tok_hdma_last[0])
                        p.wait(gpsimd, "g", tok_hdma_last[1])
                        p.wait(gpsimd, "g", tok_ffn1_s)
                        p.wait(gpsimd, "g", tok_w1)
                    if n in (2, 3):
                        p.wait(gpsimd, "g", tok_w2)
                    p.inc(gpsimd.dma_start(out=hbufs[n % 4],
                                           in_=h_d[ht * 128:(ht + 1) * 128,
                                                   rb * 256:(rb + 1) * 256]), hsem, 16)
                htok = p.tok(hsem)

                @block.vector
                def _(vector, ht=ht, n=n, htok=htok, hqo=hqo, hbufs=hbufs):
                    p.wait(vector, "v", htok)
                    p.wait(vector, "v", tok_bc2)
                    if n == 0:
                        p.wait(vector, "v", tok_w2)
                    p.inc(vector.tensor_scalar(out=wst[0][:, 0:256], in0=hbufs[n % 4],
                                               scalar1=rsh_b[:, 0:1], scalar2=256.0,
                                               op0=ALU.mult, op1=ALU.add), "v")
                    p.wait(vector, "v", p.tok("v"))
                    p.inc(vector.tensor_scalar(out=sc_i[:, 0:256], in0=wst[0][:, 0:256], scalar1=0.5,
                                               scalar2=None, op0=ALU.add), "v")
                    p.wait(vector, "v", p.tok("v"))
                    p.inc(vector.tensor_tensor(out=sc_i2[:, 0:256], in0=sc_i[:, 0:256], in1=wst[0][:, 0:256],
                                               op=ALU.subtract), "v")
                    p.wait(vector, "v", p.tok("v"))
                    p.inc(vector.scalar_tensor_tensor(
                        out=AB[:, hqo + ht * 256: hqo + (ht + 1) * 256], in0=sc_i[:, 0:256],
                        scalar=256.0, in1=sc_i2[:, 0:256], op0=ALU.subtract, op1=ALU.subtract), "v")
                tok_hq[n] = p.tok("v")

        def emit_rb_mm(rb):
            hqo = HQ0 if rb % 2 == 0 else HQ1

            @block.sync
            def _(sync, rb=rb):
                p.wait(sync, "sy", tok_xq_all)
                p.wait(sync, "sy", tok_z5.get((rb - 1, 1)))
                p.inc(sync.dma_start(out=yb[0][:], in_=y_d[rb * 256:rb * 256 + 128, :]), "yb0", 16)
                p.inc(sync.dma_start(out=yb[1][:], in_=y_d[rb * 256 + 128:(rb + 1) * 256, :]), "yb1", 16)
            ytok5a, ytok5b = p.tok("yb0"), p.tok("yb1")

            @block.tensor
            def _(tensor, rb=rb, hqo=hqo):
                p.wait(tensor, "t", tok_w2)
                if rb == 0:
                    p.wait(tensor, "t", tok_bc2)
                    p.wait(tensor, "t", tok_b2)
                p.wait(tensor, "t", tok_z5.get((rb - 1, 1)))
                for ht in range(32):
                    p.wait(tensor, "t", tok_hq[rb * 32 + ht])
                for rt in range(2):
                    for chn in range(2):
                        bank = pb[2 + 2 * rt + chn]
                        for ht in range(32):
                            dst = R2 if ht < 16 else R4
                            hto = ht if ht < 16 else ht - 16
                            p.inc(tensor.matmul(
                                bank[:, 0:512],
                                AB[:, hqo + ht * 256 + rt * 128: hqo + ht * 256 + rt * 128 + 128],
                                AB[:, dst + hto * 1024 + chn * 512: dst + hto * 1024 + (chn + 1) * 512],
                                start=(ht == 0), stop=False), "t")
                        p.inc(tensor.matmul(bank[:, 0:512], ones_b[:],
                                            AB[0:1, QSTG + chn * 512:QSTG + (chn + 1) * 512],
                                            start=False, stop=True), "t")
            tok_omm[rb] = p.tok("t")
            return ytok5a, ytok5b

        def emit_rb_tail(rb, ytok5a, ytok5b):
            for rt in range(2):
                s_ = 2 * rb + rt

                @block.vector
                def _(vector, rb=rb, rt=rt, ytok5a=ytok5a, ytok5b=ytok5b):
                    p.wait(vector, "v", ytok5a)
                    p.wait(vector, "v", ytok5b)
                    p.wait(vector, "v", tok_omm[rb])
                    p.wait(vector, "v", tok_sq5.get((rb, rt - 1)))
                    p.wait(vector, "v", tok_sq5.get((rb - 1, 1)))
                    p.wait(vector, "v", tok_outv.get((rb, rt - 1)))
                    p.wait(vector, "v", tok_outv.get((rb - 1, 1)))
                    for chn in range(2):
                        bank = pb[2 + 2 * rt + chn]
                        p.inc(vector.scalar_tensor_tensor(
                            out=zb[:, chn * 512:(chn + 1) * 512], in0=bank[:, 0:512],
                            scalar=s2_b[:, 0:1], in1=yb[rt][:, chn * 512:(chn + 1) * 512],
                            op0=ALU.mult, op1=ALU.add), "v")
                tok_z5[(rb, rt)] = p.tok("v")

                @block.vector
                def _(vector, rb=rb, rt=rt):
                    p.wait(vector, "v", tok_z5[(rb, rt)])
                    p.inc(vector.scalar_tensor_tensor(out=wst[1][:], in0=zb[:], scalar=1.0,
                                                      in1=zb[:], op0=ALU.mult, op1=ALU.mult,
                                                      accum_out=s2c[:]), "v")
                tok_sq5[(rb, rt)] = p.tok("v")

                @block.vector
                def _(vector, rb=rb, rt=rt):
                    p.wait(vector, "v", tok_z5[(rb, rt)])
                    p.inc(vector.reduce_sum(out=sum1[:], in_=zb[:], axis=AX.X), "v")
                    p.wait(vector, "v", tok_sq5[(rb, rt)])
                    p.wait(vector, "v", p.tok("v"))
                    p.inc(vector.tensor_scalar(out=mu1[:], in0=sum1[:], scalar1=1.0 / D,
                                               scalar2=None, op0=ALU.mult), "v")
                    p.wait(vector, "v", p.tok("v"))
                    p.inc(vector.scalar_tensor_tensor(out=tmp1[:], in0=mu1[:], scalar=1.0,
                                                      in1=mu1[:], op0=ALU.mult, op1=ALU.mult), "v")
                    p.inc(vector.tensor_scalar(out=var1[:], in0=s2c[:], scalar1=1.0 / D,
                                               scalar2=None, op0=ALU.mult), "v")
                    p.wait(vector, "v", p.tok("v"))
                    p.inc(vector.tensor_tensor(out=var1[:], in0=var1[:], in1=tmp1[:],
                                               op=ALU.subtract), "v")
                tok_var5 = p.tok("v")

                @block.scalar
                def _(scalar, tok_var5=tok_var5):
                    p.wait(scalar, "s", tok_var5)
                    p.inc(scalar.activation(out=sd1[:], in_=var1[:], func=AF.Sqrt,
                                            bias=eps_c[:, 0:1]), "s")
                tok_sd5 = p.tok("s")

                @block.vector
                def _(vector, rt=rt, s_=s_, tok_sd5=tok_sd5):
                    p.wait(vector, "v", tok_sd5)
                    p.wait(vector, "v", tok_outdma.get(s_ - 2))
                    p.wait(vector, "v", tok_w2)
                    p.inc(vector.reciprocal(out=rstd1[:], in_=sd1[:]), "v")
                    p.wait(vector, "v", p.tok("v"))
                    p.inc(vector.tensor_scalar(out=xob[rt][:], in0=zb[:], scalar1=mu1[:, 0:1],
                                               scalar2=rstd1[:, 0:1], op0=ALU.subtract,
                                               op1=ALU.mult), "v")
                tok_fin = p.tok("v")
                tok_outv[(rb, rt)] = tok_fin

                @block.gpsimd
                def _(gpsimd, s_=s_, rt=rt, tok_fin=tok_fin):
                    p.wait(gpsimd, "g", tok_fin)
                    p.inc(gpsimd.dma_start(out=out_d[s_ * 128:(s_ + 1) * 128, :],
                                           in_=xob[rt][:]), f"xob{rt}P", 16)
                tok_outdma[s_] = p.tok(f"xob{rt}P")

        emit_quants(0)
        for rb in range(8):
            yt = emit_rb_mm(rb)
            if rb < 7:
                emit_quants(rb + 1)
            emit_rb_tail(rb, *yt)

        @block.sync
        def _(sync):
            for sname in ["dP", "dS", "dA", "wst0", "wst1", "xob0", "xob1",
                          "yb0", "yb1", "hb0", "hb1",
                          "wst0P", "wst1P", "xob0P", "xob1P"]:
                if p.cnt[sname]:
                    sync.wait_ge(p.sems[sname], p.cnt[sname])

        @block.gpsimd
        def _(gpsimd):
            gpsimd.wait_ge(p.sems["xob0P"], p.cnt["xob0P"])
            gpsimd.wait_ge(p.sems["xob1P"], p.cnt["xob1P"])

    return nc


_NC_CACHE = {}


def kernel(L0_attn_x, L0_attn_Wq, L0_attn_Wkv, L0_attn_Wo, L0_attn_bo,
           L0_attn_ln_g, L0_attn_ln_b, L0_ffn_w1, L0_ffn_b1, L0_ffn_w2,
           L0_ffn_b2, dbg=DBG, _ret_raw=False):
    x = np.ascontiguousarray(np.asarray(L0_attn_x, dtype=np.float32))
    B = x.shape[0]
    consts = dict(
        Wq=np.ascontiguousarray(np.asarray(L0_attn_Wq, np.float32)),
        Wkv=np.ascontiguousarray(np.asarray(L0_attn_Wkv, np.float32)),
        Wo=np.ascontiguousarray(np.asarray(L0_attn_Wo, np.float32)),
        bo=np.asarray(L0_attn_bo, np.float32).reshape(1, D),
        g_rep=np.ascontiguousarray(np.broadcast_to(
            np.tile(np.asarray(L0_attn_ln_g, np.float32), H)[None, :], (128, D))),
        b_rep=np.ascontiguousarray(np.broadcast_to(
            np.tile(np.asarray(L0_attn_ln_b, np.float32), H)[None, :], (128, D))),
        w1=np.ascontiguousarray(np.asarray(L0_ffn_w1, np.float32)),
        b1c=np.ascontiguousarray(np.asarray(L0_ffn_b1, np.float32).reshape(32, 128).T),
        w2=np.ascontiguousarray(np.asarray(L0_ffn_w2, np.float32)),
        b2=np.asarray(L0_ffn_b2, np.float32).reshape(1, D),
        ident=np.eye(128, dtype=ml_dtypes.bfloat16),
        iota_r=np.arange(128, dtype=np.float32).reshape(1, 128),
        ones_f=np.ones((1, 128), np.float32),
    )
    ii = np.arange(128)
    tri = (ii[:, None] <= ii[None, :]).astype(ml_dtypes.bfloat16)
    onesm = np.ones((128, 128), ml_dtypes.bfloat16)
    zerosm = np.zeros((128, 128), ml_dtypes.bfloat16)

    key = tuple(sorted(dbg))
    if key not in _NC_CACHE:
        _NC_CACHE[key] = build(dbg)
    nc = _NC_CACHE[key]

    in_maps, metas = [], []
    for c in range(8):
        b, h = divmod(c, 2)
        out_tiles = [2 * s + h for s in range(NQ)]
        orig_tiles = [t + 12 if t < 20 else t - 20 for t in out_tiles]
        xb = x[b]
        x_q = np.concatenate([xb[t * 128:(t + 1) * 128] for t in orig_tiles], axis=0)
        x_res = np.concatenate([xb[t * 128:(t + 1) * 128] for t in out_tiles], axis=0)
        pos_q = np.concatenate([np.arange(t * 128, (t + 1) * 128) for t in orig_tiles]) \
            .astype(np.float32).reshape(1, -1)
        m = dict(consts)
        m.update(x_b=xb, x_q=np.ascontiguousarray(x_q),
                 x_res=np.ascontiguousarray(x_res), pos_q=pos_q,
                 mask_a=(tri if h == 0 else onesm), mask_b=(zerosm if h == 0 else tri))
        in_maps.append(m)
        metas.append((b, out_tiles))

    res = run_bass_kernel_spmd(nc, in_maps, core_ids=list(range(8)))
    if _ret_raw:
        return res, metas

    out = np.empty((B, SEQ, D), dtype=np.float32)
    for c in range(8):
        b, out_tiles = metas[c]
        o = np.asarray(res.results[c]["out"]).reshape(NQ, 128, D)
        for s, t in enumerate(out_tiles):
            out[b, t * 128:(t + 1) * 128] = o[s]
    return out
